# version 10
# baseline (speedup 1.0000x reference)
import numpy as np
import ml_dtypes
from contextlib import ExitStack

import concourse.bass as bass
import concourse.mybir as mybir
from concourse.bass_utils import run_bass_kernel_spmd
from concourse.alu_op_type import AluOpType as Op

F32 = mybir.dt.float32
BF16 = mybir.dt.bfloat16
I16 = mybir.dt.int16
U8 = mybir.dt.uint8
AF = mybir.ActivationFunctionType
AX = mybir.AxisListType.X
BF16NP = ml_dtypes.bfloat16

D = 2048
TOK = 4096
T = 2048
NT = TOK // 128
H = 16
HD = 128
MLP = 16384
NNT = MLP // 128
KC = D // 128
EPS = 1e-5
QS = float(1.0 / np.sqrt(np.float32(128.0)))
GC1 = 0.7978845608028654
GC2 = 0.7978845608028654 * 0.044715

NMISC = 8
MC_EPS = 0
MC_NHALF = 1
MC_QS = 2
MC_ZERO = 3

ALIBI_SLOPES = [float(np.float32(0.5) ** (np.float32(h + 1) * np.float32(0.5))) for h in range(H)]


class Ctr:
    __slots__ = ("h", "c")
    def __init__(self, h):
        self.h = h
        self.c = 0


class Em:
    def __init__(self, which, eng, ctrs):
        self.w = which
        self.e = eng
        self.c = ctrs

    def emit(self, tag, f, incs=()):
        ins = None
        if tag == self.w:
            ins = f(self.e)
            for (s, n) in incs:
                ins = ins.then_inc(self.c[s].h, n)
        for (s, n) in incs:
            self.c[s].c += n
        return ins

    def wait(self, tag, s, val=None):
        if tag == self.w:
            self.e.wait_ge(self.c[s].h, self.c[s].c if val is None else val)

    def count(self, s):
        return self.c[s].c


def run_phase(nc, ctrs, body):
    base = {k: c.c for k, c in ctrs.items()}
    with nc.Block() as block:
        def mk(tag):
            def sec(eng):
                for k, v0 in base.items():
                    ctrs[k].c = v0
                body(Em(tag, eng, ctrs))
            sec.__name__ = f"sec_{tag}"
            return sec
        block.tensor(mk('t'))
        block.vector(mk('v'))
        block.scalar(mk('a'))
        block.sync(mk('y'))
        block.gpsimd(mk('g'))


def build(stop_after=None, debug=()):
    nc = bass.Bass("TRN2", target_bir_lowering=False)
    dbg = set(debug)

    x_h = nc.dram_tensor("x", [TOK, D], F32, kind="ExternalInput").ap()
    wq_h = nc.dram_tensor("wq", [D, D], F32, kind="ExternalInput").ap()
    wk_h = nc.dram_tensor("wk", [D, D], F32, kind="ExternalInput").ap()
    wv_h = nc.dram_tensor("wv", [D, D], F32, kind="ExternalInput").ap()
    wo_h = nc.dram_tensor("wo", [D, D], F32, kind="ExternalInput").ap()
    gpk_h = nc.dram_tensor("gpk", [MLP, D // 2], U8, kind="ExternalInput").ap()
    gsc_h = nc.dram_tensor("gsc", [MLP, D // 32], F32, kind="ExternalInput").ap()
    upk_h = nc.dram_tensor("upk", [MLP, D // 2], U8, kind="ExternalInput").ap()
    usc_h = nc.dram_tensor("usc", [MLP, D // 32], F32, kind="ExternalInput").ap()
    dpk_h = nc.dram_tensor("dpk", [D, MLP // 2], U8, kind="ExternalInput").ap()
    dsc_h = nc.dram_tensor("dsc", [D, MLP // 32], F32, kind="ExternalInput").ap()
    cid_bf_h = nc.dram_tensor("cid_bf", [128, 128], BF16, kind="ExternalInput").ap()
    cid_f_h = nc.dram_tensor("cid_f", [128, 128], F32, kind="ExternalInput").ap()
    cr2_h = nc.dram_tensor("cr2", [128, 3 * 1024], F32, kind="ExternalInput").ap()
    ceb_h = nc.dram_tensor("ceb", [128, H * 16], F32, kind="ExternalInput").ap()
    cones_h = nc.dram_tensor("cones", [128, 1], BF16, kind="ExternalInput").ap()
    conesrow_h = nc.dram_tensor("conesrow", [1, 128], BF16, kind="ExternalInput").ap()
    cmisc_h = nc.dram_tensor("cmisc", [128, NMISC], F32, kind="ExternalInput").ap()

    out_h = nc.dram_tensor("out", [TOK, D], F32, kind="ExternalOutput").ap()

    sc_avT = nc.dram_tensor("sc_avT", [D, TOK], BF16, kind="Internal").ap()
    sc_x2 = nc.dram_tensor("sc_x2", [TOK, D], F32, kind="Internal").ap()
    sc_wgT = nc.dram_tensor("sc_wgT", [MLP, D], BF16, kind="Internal").ap()
    sc_wuT = nc.dram_tensor("sc_wuT", [MLP, D], BF16, kind="Internal").ap()
    sc_wdT = nc.dram_tensor("sc_wdT", [64 * 128, 4096], BF16, kind="Internal").ap()

    dbg_h = {}
    def dbg_tensor(name, shape, dtype):
        if name in dbg:
            dbg_h[name] = nc.dram_tensor("dbg_" + name, shape, dtype, kind="ExternalOutput").ap()
            return dbg_h[name]
        return None

    with ExitStack() as ctx:
        e = ctx.enter_context
        big = e(nc.sbuf_tensor("big", [128, 65536], BF16))
        cid_bf = e(nc.sbuf_tensor("sb_cid_bf", [128, 128], BF16))
        cid_f = e(nc.sbuf_tensor("sb_cid_f", [128, 128], F32))
        cones = e(nc.sbuf_tensor("sb_cones", [128, 1], BF16))
        conesrow = e(nc.sbuf_tensor("sb_conesrow", [1, 128], BF16))
        cmisc = e(nc.sbuf_tensor("sb_cmisc", [128, NMISC], F32))
        stat = e(nc.sbuf_tensor("stat", [128, 16], F32))
        sq_scr = e(nc.sbuf_tensor("sq_scr", [128, 2048], F32))
        wrow_ctx = ExitStack()
        wrow = wrow_ctx.enter_context(nc.sbuf_tensor("wrow", [128, 2 * 2048], BF16))

        sem_names = ["ly", "lyA", "lyB", "lw", "lx", "lg", "st", "stA", "stB",
                     "spA", "spB", "fwA", "fwB", "dwA", "dwB", "dwC", "wtr", "vtr",
                     "t0", "t1", "v0", "a0"]
        ctrs = {n: Ctr(e(nc.semaphore(n))) for n in sem_names}

        meta = {"ctrs": ctrs}

        def ph_consts(em):
            em.emit('y', lambda y: y.dma_start(cid_bf[:], cid_bf_h), [("ly", 16)])
            em.emit('y', lambda y: y.dma_start(cid_f[:], cid_f_h), [("ly", 16)])
            em.emit('y', lambda y: y.dma_start(cones[:], cones_h), [("ly", 16)])
            em.emit('y', lambda y: y.dma_start(conesrow[:], conesrow_h), [("ly", 16)])
            em.emit('y', lambda y: y.dma_start(cmisc[:], cmisc_h), [("ly", 16)])
            em.wait('y', "ly")
        run_phase(nc, ctrs, ph_consts)

        def fence(em):
            for tg in ('t', 'v', 'a', 'y', 'g'):
                for sm in ("st", "stA", "stB", "spA", "spB"):
                    em.wait(tg, sm)

        lctx = ExitStack()
        pst = lctx.enter_context(nc.psum_tensor("pst_l", [128, 2048], BF16))
        xin = lctx.enter_context(nc.sbuf_tensor("xin", [128, 2 * 2048], F32))
        ln1o = lctx.enter_context(nc.sbuf_tensor("ln1o", [128, 2 * 2048], BF16))

        def ln_stats(em, src_ap, rstd_col, nmean_col, n_elems, sq_slice, sqcol=5):
            em.emit('v', lambda v: v.reduce_sum(stat[:, 0:1], src_ap, AX), [("v0", 1)])
            em.emit('a', lambda a: a.activation(sq_slice, src_ap, AF.Square,
                                                accum_out=stat[:, sqcol:sqcol + 1]), [("a0", 1)])
            em.wait('v', "v0")
            em.emit('v', lambda v: v.tensor_scalar_mul(stat[:, nmean_col:nmean_col + 1],
                                                       stat[:, 0:1], -1.0 / n_elems), [("v0", 1)])
            em.wait('v', "v0")
            em.emit('v', lambda v: v.tensor_mul(stat[:, 2:3], stat[:, nmean_col:nmean_col + 1],
                                                stat[:, nmean_col:nmean_col + 1]), [("v0", 1)])
            em.wait('v', "a0")
            em.wait('v', "v0")
            em.emit('v', lambda v: v.scalar_tensor_tensor(stat[:, 3:4], stat[:, sqcol:sqcol + 1],
                                                          1.0 / n_elems, stat[:, 2:3],
                                                          Op.mult, Op.subtract), [("v0", 1)])
            em.wait('a', "v0")
            em.emit('a', lambda a: a.activation(stat[:, 6:7], stat[:, 3:4], AF.Ln,
                                                bias=cmisc[:, MC_EPS:MC_EPS + 1]), [("a0", 1)])
            em.wait('a', "a0")
            em.emit('a', lambda a: a.activation(stat[:, rstd_col:rstd_col + 1], stat[:, 6:7], AF.Exp,
                                                scale=cmisc[:, MC_NHALF:MC_NHALF + 1]), [("a0", 1)])

        def ph_ln1(em):
            for i in range(NT):
                b = i % 2
                xv = xin[:, b * 2048:(b + 1) * 2048]
                lv = ln1o[:, b * 2048:(b + 1) * 2048]
                if i >= 2:
                    em.wait('y', "t0", (i - 2 + 1) * KC)
                lsem = "lyA" if i % 2 == 0 else "lyB"
                em.emit('y', lambda y, i=i, xv=xv: y.dma_start(xv, x_h[i * 128:(i + 1) * 128, :]),
                        [(lsem, 16)])
                em.wait('v', lsem)
                em.wait('a', lsem)
                ln_stats(em, xv, rstd_col=4, nmean_col=1, n_elems=D, sq_slice=sq_scr[:])
                em.wait('v', "a0")
                em.emit('v', lambda v, xv=xv, lv=lv: v.tensor_scalar(
                    lv, xv, stat[:, 1:2], stat[:, 4:5], Op.add, Op.mult), [("v0", 1)])
                em.wait('t', "v0")
                for c in range(KC):
                    pb = c % 2
                    a0_before = em.count("a0")
                    if a0_before > 0:
                        em.wait('t', "a0", max(0, a0_before - 1))
                    em.emit('t', lambda t, lv=lv, c=c, pb=pb: t.transpose(
                        pst[:, pb * 1024:pb * 1024 + 128], lv[:, c * 128:(c + 1) * 128], cid_bf[:]),
                        [("t0", 1)])
                    em.wait('a', "t0")
                    em.emit('a', lambda a, c=c, i=i, pb=pb: a.copy(
                        big[:, c * 4096 + i * 128: c * 4096 + (i + 1) * 128],
                        pst[:, pb * 1024:pb * 1024 + 128]), [("a0", 1)])
        run_phase(nc, ctrs, ph_ln1)
        lctx.close()

        if "ln1xt" in dbg:
            tap = dbg_tensor("ln1xt", [128, 65536], BF16)
            def ph_dbg1(em):
                em.emit('y', lambda y: y.dma_start(tap, big[:]), [("st", 16)])
                em.wait('y', "st")
            run_phase(nc, ctrs, ph_dbg1)

        if stop_after == "ln1":
            _finish(nc, ctrs, out_h, big)
            return nc, dbg_h

        with ExitStack() as actx:
            ae = actx.enter_context
            ceb = ae(nc.sbuf_tensor("sb_ceb", [128, H * 16], F32))
            wT = ae(nc.sbuf_tensor("wT", [128, 3 * 2048], BF16))
            qT = ae(nc.sbuf_tensor("qT", [128, TOK], BF16))
            kT = ae(nc.sbuf_tensor("kT", [128, TOK], BF16))
            vstage = ae(nc.sbuf_tensor("vstage", [128, 512], BF16))
            vtok = ae(nc.sbuf_tensor("vtok", [128, TOK], BF16))
            ptile = ae(nc.sbuf_tensor("ptile", [128, 4 * 512], BF16))
            recbf = ae(nc.sbuf_tensor("recbf", [1, 2 * 512], BF16))
            plane = ae(nc.sbuf_tensor("plane", [128, 2 * 512], BF16))
            avsb = ae(nc.sbuf_tensor("avsb", [128, 2 * 512], BF16))
            cr2 = ae(nc.sbuf_tensor("sb_cr2", [128, 3 * 1024], F32))
            ps_a = ae(nc.psum_tensor("ps_a", [128, 1024], F32))
            ps_b = ae(nc.psum_tensor("ps_b", [128, 1024], F32))
            ps_av = ae(nc.psum_tensor("ps_av", [128, 1024], F32))
            ps_sum = ae(nc.psum_tensor("ps_sum", [128, 1024], F32))

            def ph_attn_consts(em):
                fence(em)
                em.emit('y', lambda y: y.dma_start(cr2[:], cr2_h), [("ly", 16)])
                em.emit('y', lambda y: y.dma_start(ceb[:], ceb_h), [("ly", 16)])
                em.wait('y', "ly")
            run_phase(nc, ctrs, ph_attn_consts)

            w_hs = {"q": wq_h, "k": wk_h, "v": wv_h}

            def ph_head(em, h, trk):
                stil = sq_scr
                sbank = [ps_a, ps_b]
                em.wait('t', "v0", trk.get("last_stt", 0))
                for pi in range(3):
                    pn = ("q", "k", "v")[pi]
                    gpar = (h * 3 + pi) % 2
                    em.wait('g', "wtr")
                    em.emit('g', lambda g, pn=pn, h=h, gpar=gpar: g.dma_start(
                        wrow[:, gpar * 2048:(gpar + 1) * 2048],
                        w_hs[pn][h * 128:(h + 1) * 128, :]), [("lg", 16)])
                    em.wait('y', "lg")
                    for c in range(KC):
                        em.emit('y', lambda y, pi=pi, c=c, gpar=gpar: y.dma_start(
                            wT[:, pi * 2048 + c * 128: pi * 2048 + (c + 1) * 128],
                            wrow[:, gpar * 2048 + c * 128: gpar * 2048 + (c + 1) * 128],
                            transpose=True), [("wtr", 16)])
                em.wait('t', "wtr")
                for pi, pn in enumerate(("q", "k", "v")):
                    for tc in range(8):
                        bb = tc % 2
                        a0b = em.count("a0")
                        em.wait('t', "a0", max(0, a0b - 1))
                        for c in range(KC):
                            em.emit('t', lambda t, pi=pi, tc=tc, c=c, bb=bb: t.matmul(
                                ps_a[:, bb * 512:bb * 512 + 512],
                                wT[:, pi * 2048 + c * 128: pi * 2048 + (c + 1) * 128],
                                big[:, c * 4096 + tc * 512:(c * 4096 + (tc + 1) * 512)],
                                start=(c == 0), stop=(c == KC - 1)),
                                [("t0", 1)] if c == KC - 1 else [])
                        em.wait('a', "t0")
                        if pn == "q":
                            em.emit('a', lambda a, tc=tc, bb=bb: a.activation(
                                qT[:, tc * 512:(tc + 1) * 512], ps_a[:, bb * 512:bb * 512 + 512],
                                AF.Identity, scale=cmisc[:, MC_QS:MC_QS + 1]), [("a0", 1)])
                        elif pn == "k":
                            em.emit('a', lambda a, tc=tc, bb=bb: a.copy(
                                kT[:, tc * 512:(tc + 1) * 512], ps_a[:, bb * 512:bb * 512 + 512]),
                                [("a0", 1)])
                        else:
                            em.wait('a', "vtr")
                            em.emit('a', lambda a, tc=tc, bb=bb: a.copy(
                                vstage[:], ps_a[:, bb * 512:bb * 512 + 512]), [("a0", 1)])
                            em.wait('y', "a0")
                            for u in range(4):
                                em.emit('y', lambda y, tc=tc, u=u: y.dma_start(
                                    vtok[:, tc * 512 + u * 128: tc * 512 + (u + 1) * 128],
                                    vstage[:, u * 128:(u + 1) * 128], transpose=True),
                                    [("vtr", 16)])
                em.wait('t', "a0")
                em.wait('t', "vtr")
                slope = ALIBI_SLOPES[h]
                batches = []
                for g_ in range(4):
                    nb = 2 * (g_ + 1)
                    for b_ in range(nb):
                        for s_ in range(2):
                            batches.append((s_, g_, 2 * b_, b_ == nb - 1))
                stt_pos = {}
                exp_pos = {}
                av_pos = {}
                npairs = trk.get("npairs", 0)
                for bi, (s_, g_, kt0, lastb) in enumerate(batches):
                    pair = npairs + s_ * 4 + g_
                    pb2 = bi % 2
                    sb_ = sbank[pb2]
                    didx = kt0 - 4 * g_ + 12
                    if kt0 < 4 * g_:
                        c2col = 0
                    elif kt0 == 4 * g_:
                        c2col = 1024
                    else:
                        c2col = 2048
                    if bi >= 2:
                        em.wait('t', "v0", stt_pos[bi - 2])
                    if kt0 == 0:
                        em.wait('t', "v0", trk.get(("plane_s", s_), 0))
                        em.wait('t', "v0", trk.get(("norm_s", s_), 0))
                    for half in range(2):
                        kt = kt0 + half
                        em.emit('t', lambda t, s_=s_, g_=g_, kt=kt, sb_=sb_, half=half: t.matmul(
                            sb_[:, half * 512:(half + 1) * 512],
                            kT[:, s_ * 2048 + kt * 128: s_ * 2048 + (kt + 1) * 128],
                            qT[:, s_ * 2048 + g_ * 512: s_ * 2048 + (g_ + 1) * 512],
                            start=True, stop=True), [("t0", 1)] if half == 1 else [])
                    mmp = em.count("t0")
                    em.wait('v', "t0", mmp)
                    if bi >= 2:
                        em.wait('v', "a0", exp_pos[bi - 2])
                    em.emit('v', lambda v, pb2=pb2, c2col=c2col, slope=slope, sb_=sb_: v.scalar_tensor_tensor(
                        stil[:, pb2 * 1024:(pb2 + 1) * 1024], cr2[:, c2col:c2col + 1024], slope,
                        sb_[:], Op.mult, Op.add), [("v0", 1)])
                    stt_pos[bi] = em.count("v0")
                    em.wait('a', "v0", stt_pos[bi])
                    if bi >= 2:
                        em.wait('a', "t0", av_pos[bi - 2])
                    em.emit('a', lambda a, pb2=pb2, h=h, didx=didx: a.activation(
                        ptile[:, pb2 * 1024:(pb2 + 1) * 1024], stil[:, pb2 * 1024:(pb2 + 1) * 1024],
                        AF.Exp, bias=ceb[:, h * 16 + didx: h * 16 + didx + 1]), [("a0", 1)])
                    exp_pos[bi] = em.count("a0")
                    em.wait('t', "a0", exp_pos[bi])
                    nkt = 4 * (g_ + 1)
                    for half in range(2):
                        kt = kt0 + half
                        em.emit('t', lambda t, kt=kt, pb2=pb2, half=half, s_=s_, nkt=nkt: t.matmul(
                            ps_sum[0:1, s_ * 512:s_ * 512 + 512], cones[:],
                            ptile[:, pb2 * 1024 + half * 512: pb2 * 1024 + (half + 1) * 512],
                            start=(kt == 0), stop=(kt == nkt - 1)), [])
                        em.emit('t', lambda t, s_=s_, kt=kt, pb2=pb2, half=half, nkt=nkt: t.matmul(
                            ps_av[:, s_ * 512:s_ * 512 + 512],
                            vtok[:, (s_ * 16 + kt) * 128:(s_ * 16 + kt + 1) * 128],
                            ptile[:, pb2 * 1024 + half * 512: pb2 * 1024 + (half + 1) * 512],
                            start=(kt == 0), stop=(kt == nkt - 1)),
                            [("t0", 1)] if half == 1 else [])
                    av_pos[bi] = em.count("t0")
                    if lastb:
                        pp = s_
                        em.wait('v', "t0", av_pos[bi])
                        def _recip(v, pp=pp):
                            with nc.allow_low_precision(reason="softmax 1/S bf16; LN scale-inv"):
                                return v.reciprocal(recbf[:, pp * 512:(pp + 1) * 512],
                                                    ps_sum[0:1, pp * 512:pp * 512 + 512])
                        em.emit('v', _recip, [("v0", 1)])
                        rp = em.count("v0")
                        em.wait('t', "v0", rp)
                        em.emit('t', lambda t, pp=pp: t.matmul(
                            ps_sum[:, pp * 512:pp * 512 + 512], conesrow[:],
                            recbf[:, pp * 512:(pp + 1) * 512], start=True, stop=True), [("t0", 1)])
                        plp = em.count("t0")
                        em.wait('v', "t0", plp)
                        em.emit('v', lambda v, pp=pp: v.tensor_copy(
                            plane[:, pp * 512:(pp + 1) * 512], ps_sum[:, pp * 512:pp * 512 + 512]),
                            [("v0", 1)])
                        trk[("plane_s", s_)] = em.count("v0")
                        em.wait('v', "v0", trk[("plane_s", s_)])
                        em.wait('v', "stA" if pp == 0 else "stB")
                        em.emit('v', lambda v, pp=pp: v.tensor_mul(
                            avsb[:, pp * 512:(pp + 1) * 512],
                            ps_av[:, pp * 512:pp * 512 + 512],
                            plane[:, pp * 512:(pp + 1) * 512]), [("v0", 1)])
                        trk[("norm_s", s_)] = em.count("v0")
                        em.wait('y', "v0", trk[("norm_s", s_)])
                        em.emit('y', lambda y, h=h, s_=s_, g_=g_, pp=pp: y.dma_start(
                            sc_avT[h * 128:(h + 1) * 128,
                                   s_ * 2048 + g_ * 512: s_ * 2048 + (g_ + 1) * 512],
                            avsb[:, pp * 512:(pp + 1) * 512]),
                            [("stA" if pp == 0 else "stB", 16)])
                trk["npairs"] = npairs + 8
                trk["last_stt"] = stt_pos[len(batches) - 1]

            attn_trk = {}
            for h in range(H):
                res_trk = {}
                def head_body(em, h=h, res_trk=res_trk):
                    trk = dict(attn_trk)
                    ph_head(em, h, trk)
                    res_trk.clear()
                    res_trk.update(trk)
                run_phase(nc, ctrs, head_body)
                attn_trk = res_trk

        if "avt" in dbg:
            tap = dbg_tensor("avt", [D, TOK], BF16)
            def ph_dbga(em):
                for r in range(4):
                    em.emit('y', lambda y, r=r: y.dma_start(
                        tap[r * 512:(r + 1) * 512, :], sc_avT[r * 512:(r + 1) * 512, :]),
                        [("st", 16)])
                em.wait('y', "st")
            run_phase(nc, ctrs, ph_dbga)

        if stop_after == "attn":
            _finish(nc, ctrs, out_h, big)
            return nc, dbg_h

        bigf = big[:].bitcast(F32)
        with ExitStack() as wctx:
            we = wctx.enter_context
            projst = we(nc.sbuf_tensor("projst", [128, 2 * 512], F32))
            x2t = we(nc.sbuf_tensor("x2t", [128, 2048], F32))
            xres = we(nc.sbuf_tensor("xres", [128, 2048], F32))
            tmpn = we(nc.sbuf_tensor("tmpn", [128, 2048], F32))
            ps_p = we(nc.psum_tensor("ps_p", [128, 1024], F32))
            ps_tr = we(nc.psum_tensor("ps_tr", [128, 1024], F32))

            def ph_wo_prep(em):
                fence(em)
                for m in range(16):
                    gpar = m % 2
                    em.wait('g', "wtr")
                    em.emit('g', lambda g, m=m, gpar=gpar: g.dma_start(
                        wrow[:, gpar * 2048:(gpar + 1) * 2048],
                        wo_h[m * 128:(m + 1) * 128, :]), [("lg", 16)])
                    em.wait('y', "lg")
                    for c in range(KC):
                        em.emit('y', lambda y, c=c, m=m, gpar=gpar: y.dma_start(
                            big[:, 8192 + c * 2048 + m * 128: 8192 + c * 2048 + (m + 1) * 128],
                            wrow[:, gpar * 2048 + c * 128: gpar * 2048 + (c + 1) * 128],
                            transpose=True), [("wtr", 16)])
                em.wait('t', "wtr")
                em.wait('v', "wtr")
            run_phase(nc, ctrs, ph_wo_prep)

            def ph_wo_tc(em, tc):
                em.wait('y', "stA")
                em.wait('y', "stB")
                em.emit('y', lambda y, tc=tc: y.dma_start(
                    big[:, 0:8192],
                    sc_avT.rearrange("(c p) t -> p c t", p=128)[:, :, tc * 512:(tc + 1) * 512]),
                    [("lw", 16)])
                em.wait('t', "lw")
                for m in range(16):
                    bb = m % 2
                    a0b = em.count("a0")
                    em.wait('t', "a0", max(0, a0b - 1))
                    for dch in range(16):
                        em.emit('t', lambda t, m=m, dch=dch, bb=bb: t.matmul(
                            ps_p[:, bb * 512:bb * 512 + 512],
                            big[:, 8192 + dch * 2048 + m * 128: 8192 + dch * 2048 + (m + 1) * 128],
                            big[:, dch * 512:(dch + 1) * 512],
                            start=(dch == 0), stop=(dch == KC - 1)),
                            [("t0", 1)] if dch == KC - 1 else [])
                    em.wait('a', "t0")
                    em.emit('a', lambda a, m=m, bb=bb: a.copy(
                        projst[:, bb * 512:(bb + 1) * 512], ps_p[:, bb * 512:bb * 512 + 512]),
                        [("a0", 1)])
                    for u in range(4):
                        pb = u % 2
                        v0b = em.count("v0")
                        em.wait('t', "v0", max(0, v0b - 1))
                        em.wait('t', "a0")
                        em.emit('t', lambda t, u=u, pb=pb, bb=bb: t.transpose(
                            ps_tr[:, pb * 512:pb * 512 + 128],
                            projst[:, bb * 512 + u * 128: bb * 512 + (u + 1) * 128], cid_f[:]),
                            [("t0", 1)])
                        em.wait('v', "t0")
                        em.emit('v', lambda v, u=u, m=m, pb=pb: v.tensor_copy(
                            bigf[:, 20480 + u * 2048 + m * 128: 20480 + u * 2048 + (m + 1) * 128],
                            ps_tr[:, pb * 512:pb * 512 + 128]), [("v0", 1)])
                for u in range(4):
                    row0 = tc * 512 + u * 128
                    xo = bigf[:, 20480 + u * 2048: 20480 + (u + 1) * 2048]
                    em.wait('y', "v0")
                    em.emit('y', lambda y, row0=row0: y.dma_start(
                        xres[:], x_h[row0:row0 + 128, :]), [("lx", 16)])
                    em.wait('v', "v0")
                    em.wait('a', "v0")
                    ln_stats(em, xo, rstd_col=4, nmean_col=1, n_elems=D, sq_slice=sq_scr[:])
                    em.wait('v', "a0")
                    em.emit('v', lambda v, xo=xo: v.tensor_scalar(
                        tmpn[:], xo, stat[:, 1:2], stat[:, 4:5], Op.add, Op.mult), [("v0", 1)])
                    em.wait('v', "lx")
                    em.wait('v', "st")
                    em.wait('v', "v0")
                    em.emit('v', lambda v: v.tensor_add(x2t[:], tmpn[:], xres[:]), [("v0", 1)])
                    em.wait('y', "v0")
                    em.emit('y', lambda y, row0=row0: y.dma_start(
                        sc_x2[row0:row0 + 128, :], x2t[:]), [("st", 16)])

            for tc in range(8):
                run_phase(nc, ctrs, lambda em, tc=tc: ph_wo_tc(em, tc))

        if "x2" in dbg:
            tap = dbg_tensor("x2", [TOK, D], F32)
            def ph_dbgx2(em):
                for r in range(8):
                    em.emit('y', lambda y, r=r: y.dma_start(
                        tap[r * 512:(r + 1) * 512, :], sc_x2[r * 512:(r + 1) * 512, :]),
                        [("st", 16)])
                em.wait('y', "st")
            run_phase(nc, ctrs, ph_dbgx2)

        if stop_after == "wo":
            _finish(nc, ctrs, out_h, big)
            return nc, dbg_h

        wrow_ctx.close()

        bigI = big[:].bitcast(I16)
        q_pkb = [bigI[:, 0:2048], bigI[:, 2048:4096]]
        q_gb = bigI[:, 8192:12288]
        q_sb = bigI[:, 12288:16384]
        q_mg2 = big[:, 16384:20480]
        q_mg = big[:, 20480:24576]
        q_ws = big[:, 24576:28672]
        q_wsc2 = [big[:, 28672:32768], big[:, 45056:49152]]
        q_wt = big[:, 32768:40960]
        q_sx = big[:, 40960:45056]
        a0_done = {}

        with ExitStack() as qctx:
            qe = qctx.enter_context
            pst = qe(nc.psum_tensor("pst_q", [128, 2048], BF16))
            scl = qe(nc.sbuf_tensor("scl", [128, 1024], F32))
            scl4 = qe(nc.sbuf_tensor("scl4", [128, 1024], F32))

            class Chain:
                __slots__ = ("ops", "pos", "i")
                def __init__(self):
                    self.ops = []
                    self.pos = None
                    self.i = 0

            def decode_chain(pkb_ap, which_half, gb, sb, mg2, mg, ws, sx_ap, out_ap):
                ch = Chain()
                if which_half == 0:
                    ch.ops.append(lambda v, a=pkb_ap, o=gb: v.tensor_scalar(
                        o, a, 6, 0x3C0, Op.logical_shift_left, Op.bitwise_and))
                else:
                    ch.ops.append(lambda v, a=pkb_ap, o=gb: v.tensor_scalar(
                        o, a, 4, 6, Op.logical_shift_right, Op.logical_shift_left))
                ch.ops.append(lambda v, i=gb, o=sb: v.tensor_scalar(
                    o, i, 0x200, 6, Op.bitwise_and, Op.logical_shift_left))
                ch.ops.append(lambda v, i=gb: v.tensor_scalar(
                    i, i, 0x1C0, 0x3E00, Op.bitwise_and, Op.bitwise_or))
                ch.ops.append(lambda v, i=gb, o=mg2: v.tensor_scalar(
                    o, i.bitcast(BF16), 2.0, -0.25, Op.mult, Op.add))
                ch.ops.append(lambda v, i=gb, m2=mg2, o=mg: v.tensor_tensor(
                    o, i.bitcast(BF16), m2, Op.min))
                ch.ops.append(lambda v, m=mg, sb_=sb, o=ws: v.tensor_tensor(
                    o.bitcast(I16), m.bitcast(I16), sb_, Op.bitwise_or))
                ch.ops.append(lambda v, w=ws, sx=sx_ap, o=out_ap: v.tensor_tensor(
                    o, w, sx, Op.mult))
                return ch

            def run_chains(em, chains):
                live = True
                while live:
                    live = False
                    for ch in chains:
                        if ch.i < len(ch.ops):
                            live = True
                            if ch.pos is not None:
                                em.wait('v', "v0", ch.pos)
                            em.emit('v', ch.ops[ch.i], [("v0", 1)])
                            ch.pos = em.count("v0")
                            ch.i += 1

            v0_done = {}
            a0_grp = []

            def emit_groups(em, n_groups, src_par, wt_off, spsem):
                for grp in range(n_groups):
                    pb = grp % 2
                    if len(a0_grp) >= 2:
                        em.wait('t', "a0", a0_grp[-2])
                    for i in range(8):
                        em.emit('t', lambda t, grp=grp, i=i, pb=pb, sp_=src_par: t.transpose(
                            pst[:, pb * 1024 + i * 128: pb * 1024 + (i + 1) * 128],
                            q_wsc2[sp_][:, grp * 1024 + i * 128: grp * 1024 + (i + 1) * 128],
                            cid_bf[:]), [("t0", 1)] if i == 7 else [])
                    em.wait('a', "t0")
                    em.wait('a', spsem)
                    em.emit('a', lambda a, grp=grp, pb=pb, wo=wt_off: a.copy(
                        q_wt[:, wo + grp * 1024: wo + (grp + 1) * 1024],
                        pst[:, pb * 1024: (pb + 1) * 1024]), [("a0", 1)])
                    a0_grp.append(em.count("a0"))

            def gu_loads(em, nt):
                par = nt % 2
                lsem = "lyA" if par == 0 else "lyB"
                ssem = "lw" if par == 0 else "lx"
                if nt >= 1:
                    em.wait('g', "v0", v0_done.get(("gu", nt - 1), 0))
                    em.wait('y', "v0", v0_done.get(("gu", nt - 1), 0))
                em.emit('g', lambda g, nt=nt, par=par: g.dma_start(
                    q_pkb[par][:, 0:1024], gpk_h[nt * 128:(nt + 1) * 128, :]), [(lsem, 16)])
                em.emit('g', lambda g, nt=nt, par=par: g.dma_start(
                    q_pkb[par][:, 1024:2048], upk_h[nt * 128:(nt + 1) * 128, :]), [(lsem, 16)])
                em.emit('y', lambda y, nt=nt, par=par: y.dma_start(
                    scl[:, par * 128: par * 128 + 64], gsc_h[nt * 128:(nt + 1) * 128, :]),
                    [(ssem, 16)])
                em.emit('y', lambda y, nt=nt, par=par: y.dma_start(
                    scl[:, par * 128 + 64: par * 128 + 128], usc_h[nt * 128:(nt + 1) * 128, :]),
                    [(ssem, 16)])

            def ph_dequant_gu(em, nt):
                par = nt % 2
                lsem = "lyA" if par == 0 else "lyB"
                ssem = "lw" if par == 0 else "lx"
                if nt + 1 < NNT:
                    gu_loads(em, nt + 1)
                em.wait('v', lsem)
                em.wait('v', ssem)
                em.emit('v', lambda v, par=par: v.tensor_scalar_mul(
                    scl4[:, par * 128:par * 128 + 128], scl[:, par * 128:par * 128 + 128], 4.0),
                    [("v0", 1)])
                em.wait('v', "v0")
                for ti in range(2):
                    sc_src = scl4[:, par * 128 + ti * 64: par * 128 + (ti + 1) * 64]
                    em.emit('v', lambda v, ti=ti, sc_src=sc_src: v.tensor_copy(
                        q_sx[:, ti * 2048: ti * 2048 + 1024].rearrange("p (n o) -> p n o", o=16),
                        sc_src.rearrange("p (n o) -> p n o", o=1).broadcast_to([128, 64, 16])),
                        [("v0", 1)])
                em.wait('v', "v0")
                em.wait('v', "a0", a0_done.get(("gu", nt - 2), 0))
                q_wsc = q_wsc2[par]
                chains = []
                for ti in range(2):
                    for hf in range(2):
                        q = ti * 2 + hf
                        chains.append(decode_chain(
                            q_pkb[par][:, ti * 1024:(ti + 1) * 1024], hf,
                            q_gb[:, q * 1024:(q + 1) * 1024], q_sb[:, q * 1024:(q + 1) * 1024],
                            q_mg2[:, q * 1024:(q + 1) * 1024],
                            q_mg[:, q * 1024:(q + 1) * 1024], q_ws[:, q * 1024:(q + 1) * 1024],
                            q_sx[:, ti * 2048: ti * 2048 + 1024],
                            q_wsc[:, (ti * 2 + hf) * 1024:(ti * 2 + hf + 1) * 1024]))
                run_chains(em, chains)
                v0_done[("gu", nt)] = em.count("v0")
                em.wait('t', "v0")
                wt_off = par * 4096
                spsem = "spA" if par == 0 else "spB"
                for ti in range(2):
                    for grp in range(2):
                        pb = (ti * 2 + grp) % 2
                        if len(a0_grp) >= 2:
                            em.wait('t', "a0", a0_grp[-2])
                        for i in range(8):
                            cc = grp * 8 + i
                            c, hf = cc // 2, cc % 2
                            em.emit('t', lambda t, q_wsc=q_wsc, ti=ti, c=c, hf=hf, i=i, pb=pb: t.transpose(
                                pst[:, pb * 1024 + i * 128: pb * 1024 + (i + 1) * 128],
                                q_wsc[:, ti * 2048 + hf * 1024 + c * 128:
                                      ti * 2048 + hf * 1024 + (c + 1) * 128],
                                cid_bf[:]), [("t0", 1)] if i == 7 else [])
                        em.wait('a', "t0")
                        em.wait('a', spsem)
                        em.emit('a', lambda a, ti=ti, grp=grp, pb=pb, wo=wt_off: a.copy(
                            q_wt[:, wo + ti * 2048 + grp * 1024: wo + ti * 2048 + (grp + 1) * 1024],
                            pst[:, pb * 1024:(pb + 1) * 1024]), [("a0", 1)])
                        a0_grp.append(em.count("a0"))
                a0_done[("gu", nt)] = em.count("a0")
                em.wait('y', "a0")
                em.emit('y', lambda y, nt=nt, wo=wt_off: y.dma_start(
                    sc_wgT[nt * 128:(nt + 1) * 128, :], q_wt[:, wo:wo + 2048]), [(spsem, 16)])
                em.emit('y', lambda y, nt=nt, wo=wt_off: y.dma_start(
                    sc_wuT[nt * 128:(nt + 1) * 128, :], q_wt[:, wo + 2048:wo + 4096]), [(spsem, 16)])

            def wd_loads(em, j, sl):
                gi = j * 4 + sl
                par = gi % 2
                lsem = "lyA" if par == 0 else "lyB"
                if gi >= 1:
                    em.wait('g', "v0", v0_done.get(("wd", gi - 1), v0_done.get(("gu", NNT - 1), 0)))
                em.emit('g', lambda g, j=j, sl=sl, par=par: g.dma_start(
                    q_pkb[par][:], dpk_h[j * 128:(j + 1) * 128, sl * 2048:(sl + 1) * 2048]),
                    [(lsem, 16)])
                if sl == 0:
                    ssem = "lw" if (j % 2) == 0 else "lx"
                    em.wait('y', "v0", v0_done.get(("wdscl", j - 1),
                                                   v0_done.get(("gu", NNT - 1), 0)))
                    em.emit('y', lambda y, j=j: y.dma_start(
                        scl[:, (j % 2) * 512:(j % 2) * 512 + 512],
                        dsc_h[j * 128:(j + 1) * 128, :]), [(ssem, 16)])

            def ph_dequant_wd(em, j, sl):
                gi = j * 4 + sl
                par = gi % 2
                lsem = "lyA" if par == 0 else "lyB"
                if gi + 1 < H * 4:
                    wd_loads(em, (gi + 1) // 4, (gi + 1) % 4)
                if sl == 0:
                    ssem = "lw" if (j % 2) == 0 else "lx"
                    em.wait('v', ssem)
                    em.emit('v', lambda v, j=j: v.tensor_scalar_mul(
                        scl4[:, (j % 2) * 512:(j % 2) * 512 + 512],
                        scl[:, (j % 2) * 512:(j % 2) * 512 + 512], 4.0), [("v0", 1)])
                    v0_done[("wdscl", j)] = em.count("v0")
                    em.wait('v', "v0")
                em.wait('v', lsem)
                sc_src = scl4[:, (j % 2) * 512 + sl * 128:(j % 2) * 512 + (sl + 1) * 128]
                em.emit('v', lambda v, sc_src=sc_src: v.tensor_copy(
                    q_sx[:, 0:2048].rearrange("p (n o) -> p n o", o=16),
                    sc_src.rearrange("p (n o) -> p n o", o=1).broadcast_to([128, 128, 16])),
                    [("v0", 1)])
                em.wait('v', "v0")
                em.wait('v', "a0", a0_done.get(("wd", gi - 2), a0_done.get(("gu", NNT - 1), 0)))
                q_wsc = q_wsc2[par]
                ev = q_wsc.rearrange("p (j two) -> p j two", two=2)[:, :, 0]
                od = q_wsc.rearrange("p (j two) -> p j two", two=2)[:, :, 1]
                chains = []
                for hf, out_ap in ((0, ev), (1, od)):
                    chains.append(decode_chain(
                        q_pkb[par][:], hf,
                        q_gb[:, hf * 2048:(hf + 1) * 2048], q_sb[:, hf * 2048:(hf + 1) * 2048],
                        q_mg2[:, hf * 2048:(hf + 1) * 2048], q_mg[:, hf * 2048:(hf + 1) * 2048],
                        q_ws[:, hf * 2048:(hf + 1) * 2048], q_sx[:, 0:2048], out_ap))
                run_chains(em, chains)
                v0_done[("wd", gi)] = em.count("v0")
                em.wait('t', "v0")
                wt_off = par * 4096
                spsem = "spA" if par == 0 else "spB"
                emit_groups(em, 4, par, wt_off, spsem)
                a0_done[("wd", gi)] = em.count("a0")
                em.wait('y', "a0")
                base = gi * 128
                em.emit('y', lambda y, base=base, wo=wt_off: y.dma_start(
                    sc_wdT[base:base + 128, :], q_wt[:, wo:wo + 4096]), [(spsem, 16)])

            def ph_dequant(em):
                fence(em)
                gu_loads(em, 0)
                for nt in range(NNT):
                    ph_dequant_gu(em, nt)
                wd_loads(em, 0, 0)
                for j in range(H):
                    for sl in range(4):
                        ph_dequant_wd(em, j, sl)
            run_phase(nc, ctrs, ph_dequant)

        if "wgt" in dbg:
            tap = dbg_tensor("wgt", [MLP, D], BF16)
            def ph_dbgw(em):
                for r in range(16):
                    em.emit('y', lambda y, r=r: y.dma_start(
                        tap[r * 1024:(r + 1) * 1024, :], sc_wgT[r * 1024:(r + 1) * 1024, :]),
                        [("st", 16)])
                em.wait('y', "st")
            run_phase(nc, ctrs, ph_dbgw)
        if "wdt" in dbg:
            tap = dbg_tensor("wdt", [8192, 4096], BF16)
            def ph_dbgwd(em):
                for r in range(16):
                    em.emit('y', lambda y, r=r: y.dma_start(
                        tap[r * 512:(r + 1) * 512, :], sc_wdT[r * 512:(r + 1) * 512, :]),
                        [("st", 16)])
                em.wait('y', "st")
            run_phase(nc, ctrs, ph_dbgwd)

        if stop_after == "dequant":
            _finish(nc, ctrs, out_h, big)
            return nc, dbg_h

        ln3xT = e(nc.sbuf_tensor("ln3xT", [128, 16 * 512], BF16))
        v0_hist = {}
        a0_hist = {}
        for tc in range(8):
            with ExitStack() as fa:
                fe = fa.enter_context
                pst = fe(nc.psum_tensor(f"pst_f_{tc}", [128, 2048], BF16))
                x2in = fe(nc.sbuf_tensor(f"x2in_{tc}", [128, 2 * 2048], F32))
                ln3o = fe(nc.sbuf_tensor(f"ln3o_{tc}", [128, 2 * 2048], BF16))

                def ph_ln3(em, tc=tc):
                    fence(em)
                    for u in range(4):
                        b = u % 2
                        lsem = "lyA" if b == 0 else "lyB"
                        xv = x2in[:, b * 2048:(b + 1) * 2048]
                        lv = ln3o[:, b * 2048:(b + 1) * 2048]
                        em.wait('y', "v0")
                        row0 = tc * 512 + u * 128
                        em.emit('y', lambda y, row0=row0, xv=xv: y.dma_start(
                            xv, sc_x2[row0:row0 + 128, :]), [(lsem, 16)])
                        em.wait('v', lsem)
                        em.wait('a', lsem)
                        ln_stats(em, xv, rstd_col=4, nmean_col=1, n_elems=D, sq_slice=sq_scr[:],
                                 sqcol=5 + (u % 2) * 2)
                        em.wait('v', "a0")
                        em.wait('v', "t0")
                        em.emit('v', lambda v, xv=xv, lv=lv: v.tensor_scalar(
                            lv, xv, stat[:, 1:2], stat[:, 4:5], Op.add, Op.mult), [("v0", 1)])
                        em.wait('t', "v0")
                        for cc in range(16):
                            c, par = cc // 2, cc % 2
                            pb = cc % 2
                            a0b = em.count("a0")
                            em.wait('t', "a0", a0b if (u == 0 and cc == 0) else max(0, a0b - 1))
                            em.emit('t', lambda t, lv=lv, c=c, par=par, pb=pb: t.transpose(
                                pst[:, pb * 1024:pb * 1024 + 128],
                                lv[:, 256 * c + par: 256 * (c + 1): 2], cid_bf[:]), [("t0", 1)])
                            em.wait('a', "t0")
                            em.emit('a', lambda a, cc=cc, u=u, pb=pb: a.copy(
                                ln3xT[:, cc * 512 + u * 128: cc * 512 + (u + 1) * 128],
                                pst[:, pb * 1024:pb * 1024 + 128]), [("a0", 1)])
                run_phase(nc, ctrs, ph_ln3)

            with ExitStack() as fb:
                fe = fb.enter_context
                wgbuf = fe(nc.sbuf_tensor(f"wgbuf_{tc}", [128, 2 * 2048], BF16))
                wubuf = fe(nc.sbuf_tensor(f"wubuf_{tc}", [128, 2 * 2048], BF16))
                gsc = fe(nc.sbuf_tensor(f"gsc_{tc}", [128, 4096], F32))
                ps_g = fe(nc.psum_tensor(f"ps_g_{tc}", [128, 1024], F32))
                ps_u = fe(nc.psum_tensor(f"ps_u_{tc}", [128, 1024], F32))

                def ph_gateup(em, tc=tc):
                    fence(em)
                    v0_gelu = {}
                    t1_0 = em.count("t1")
                    for nt in range(NNT):
                        rb = nt % 2
                        for pf in ([0, 1] if nt == 0 else [nt + 1]):
                            if pf < NNT:
                                fsem = "fwA" if pf % 2 == 0 else "fwB"
                                if pf >= 2:
                                    em.wait('y', "t1", t1_0 + pf - 1)
                                em.emit('y', lambda y, pf=pf: y.dma_start(
                                    wgbuf[:, (pf % 2) * 2048:(pf % 2 + 1) * 2048],
                                    sc_wgT[pf * 128:(pf + 1) * 128, :]), [(fsem, 16)])
                                em.emit('y', lambda y, pf=pf: y.dma_start(
                                    wubuf[:, (pf % 2) * 2048:(pf % 2 + 1) * 2048],
                                    sc_wuT[pf * 128:(pf + 1) * 128, :]), [(fsem, 16)])
                        em.wait('t', "fwA" if rb == 0 else "fwB")
                        if nt >= 2:
                            em.wait('t', "v0", v0_gelu[nt - 2])
                        for cc in range(16):
                            em.emit('t', lambda t, cc=cc, rb=rb: t.matmul(
                                ps_g[:, rb * 512:rb * 512 + 512],
                                wgbuf[:, rb * 2048 + cc * 128: rb * 2048 + (cc + 1) * 128],
                                ln3xT[:, cc * 512:(cc + 1) * 512],
                                start=(cc == 0), stop=(cc == 15)), [])
                        for cc in range(16):
                            em.emit('t', lambda t, cc=cc, rb=rb: t.matmul(
                                ps_u[:, rb * 512:rb * 512 + 512],
                                wubuf[:, rb * 2048 + cc * 128: rb * 2048 + (cc + 1) * 128],
                                ln3xT[:, cc * 512:(cc + 1) * 512],
                                start=(cc == 0), stop=(cc == 15)),
                                [("t1", 1)] if cc == 15 else [])
                        gb_ = (nt % 2) * 2048
                        s0 = gsc[:, gb_ + 0:gb_ + 512]
                        s1 = gsc[:, gb_ + 512:gb_ + 1024]
                        s2 = gsc[:, gb_ + 1024:gb_ + 1536]
                        s3 = gsc[:, gb_ + 1536:gb_ + 2048]
                        em.wait('a', "t1")
                        em.emit('a', lambda a, rb=rb: a.activation(s0, ps_g[:, rb * 512:rb * 512 + 512],
                                                                   AF.Square), [("a0", 1)])
                        em.wait('v', "a0")
                        em.emit('v', lambda v: v.tensor_scalar(s1, s0, GC2, GC1, Op.mult, Op.add),
                                [("v0", 1)])
                        em.wait('v', "v0")
                        em.emit('v', lambda v, rb=rb: v.tensor_tensor(
                            s2, ps_g[:, rb * 512:rb * 512 + 512], s1, Op.mult), [("v0", 1)])
                        em.wait('a', "v0")
                        em.emit('a', lambda a: a.activation(s3, s2, AF.Tanh), [("a0", 1)])
                        em.wait('v', "a0")
                        em.emit('v', lambda v: v.tensor_scalar(s1, s3, 0.5, 0.5, Op.mult, Op.add),
                                [("v0", 1)])
                        em.wait('v', "v0")
                        em.emit('v', lambda v, rb=rb: v.tensor_tensor(
                            s2, ps_g[:, rb * 512:rb * 512 + 512], s1, Op.mult), [("v0", 1)])
                        em.wait('v', "v0")
                        em.emit('v', lambda v, rb=rb, nt=nt: v.tensor_tensor(
                            big[:, nt * 512:(nt + 1) * 512], ps_u[:, rb * 512:rb * 512 + 512],
                            s2, Op.mult), [("v0", 1)])
                        v0_gelu[nt] = em.count("v0")
                run_phase(nc, ctrs, ph_gateup)

            with ExitStack() as fc:
                fe = fc.enter_context
                wdring = fe(nc.sbuf_tensor(f"wdring_{tc}", [128, 3 * 2048], BF16))
                dstage = fe(nc.sbuf_tensor(f"dstage_{tc}", [128, 2 * 512], F32))
                xo2 = fe(nc.sbuf_tensor(f"xo2_{tc}", [128, 2 * 2048], F32))
                x2res = fe(nc.sbuf_tensor(f"x2res_{tc}", [128, 2048], F32))
                outt = fe(nc.sbuf_tensor(f"outt_{tc}", [128, 2048], F32))
                ps_d = fe(nc.psum_tensor(f"ps_d_{tc}", [128, 1024], F32))
                ps_t2 = fe(nc.psum_tensor(f"ps_t2_{tc}", [128, 1024], F32))

                def ph_down(em, tc=tc):
                    tmpn2 = sq_scr
                    ln3f = ln3xT[:].bitcast(F32)
                    def xo_ap(u, lo, hi):
                        if u < 2:
                            return ln3f[:, u * 2048 + lo: u * 2048 + hi]
                        return xo2[:, (u - 2) * 2048 + lo: (u - 2) * 2048 + hi]
                    a0_evac = {}
                    t1_0 = em.count("t1")
                    fence(em)
                    for j in range(H):
                        jb = j % 2
                        for bt in range(8):
                            gi = j * 8 + bt
                            for pf in ([0, 1, 2] if gi == 0 else [gi + 2]):
                                if pf < H * 8:
                                    pj, pbt = pf // 8, pf % 8
                                    ps_ = pj * 4 + pbt // 2
                                    dsem = ("dwA", "dwB", "dwC")[pf % 3]
                                    if pf >= 3:
                                        em.wait('y', "t1", t1_0 + pf - 2)
                                    em.emit('y', lambda y, ps_=ps_, pbt=pbt, pf=pf: y.dma_start(
                                        wdring[:, (pf % 3) * 2048:(pf % 3 + 1) * 2048],
                                        sc_wdT[ps_ * 128:(ps_ + 1) * 128,
                                               (pbt % 2) * 2048:(pbt % 2) * 2048 + 2048]),
                                        [(dsem, 16)])
                            em.wait('t', ("dwA", "dwB", "dwC")[gi % 3])
                            if j >= 2 and bt == 0:
                                em.wait('t', "a0", a0_evac[j - 2])
                            for u in range(16):
                                i = bt * 16 + u
                                em.emit('t', lambda t, u=u, i=i, gi=gi, jb=jb: t.matmul(
                                    ps_d[:, jb * 512:jb * 512 + 512],
                                    wdring[:, (gi % 3) * 2048 + u * 128:(gi % 3) * 2048 + (u + 1) * 128],
                                    big[:, i * 512:(i + 1) * 512],
                                    start=(i == 0), stop=(i == NNT - 1)),
                                    [("t1", 1)] if u == 15 else [])
                        em.wait('a', "t1")
                        em.emit('a', lambda a, jb=jb: a.copy(
                            dstage[:, jb * 512:(jb + 1) * 512], ps_d[:, jb * 512:jb * 512 + 512]),
                            [("a0", 1)])
                        a0_evac[j] = em.count("a0")
                        for u in range(4):
                            pb = u % 2
                            v0b = em.count("v0")
                            em.wait('t', "v0", v0b if (j == 0 and u == 0) else max(0, v0b - 1))
                            em.wait('t', "a0")
                            em.emit('t', lambda t, u=u, jb=jb, pb=pb: t.transpose(
                                ps_t2[:, pb * 512:pb * 512 + 128],
                                dstage[:, jb * 512 + u * 128: jb * 512 + (u + 1) * 128], cid_f[:]),
                                [("t0", 1)])
                            em.wait('v', "t0")
                            em.emit('v', lambda v, u=u, j=j, pb=pb: v.tensor_copy(
                                xo_ap(u, j * 128, (j + 1) * 128),
                                ps_t2[:, pb * 512:pb * 512 + 128]), [("v0", 1)])
                    for u in range(4):
                        row0 = tc * 512 + u * 128
                        xo = xo_ap(u, 0, 2048)
                        em.wait('y', "v0")
                        em.emit('y', lambda y, row0=row0: y.dma_start(
                            x2res[:], sc_x2[row0:row0 + 128, :]), [("lx", 16)])
                        em.wait('v', "v0")
                        em.wait('a', "v0")
                        ln_stats(em, xo, rstd_col=4, nmean_col=1, n_elems=D, sq_slice=sq_scr[:],
                                 sqcol=5 + (u % 2) * 2)
                        em.wait('v', "a0")
                        em.emit('v', lambda v, xo=xo: v.tensor_scalar(
                            tmpn2[:, 0:2048], xo, stat[:, 1:2], stat[:, 4:5], Op.add, Op.mult), [("v0", 1)])
                        em.wait('v', "lx")
                        em.wait('v', "st")
                        em.wait('v', "v0")
                        em.emit('v', lambda v: v.tensor_add(outt[:], tmpn2[:, 0:2048], x2res[:]), [("v0", 1)])
                        em.wait('y', "v0")
                        em.emit('y', lambda y, row0=row0: y.dma_start(
                            out_h[row0:row0 + 128, :], outt[:]), [("st", 16)])
                run_phase(nc, ctrs, ph_down)

        def ph_fin(em):
            em.wait('y', "st")
            em.wait('t', "st")
        run_phase(nc, ctrs, ph_fin)
        return nc, dbg_h


def _finish(nc, ctrs, out_h, big):
    def ph(em):
        em.emit('y', lambda y: y.dma_start(out_h[0:128, :], big[:, 0:4096].bitcast(F32)),
                [("st", 16)])
        em.wait('y', "st")
    run_phase(nc, ctrs, ph)


def _consts():
    cid_bf = np.eye(128, dtype=BF16NP)
    cid_f = np.eye(128, dtype=np.float32)
    p = np.arange(128, dtype=np.float32)[:, None]
    j = np.arange(512, dtype=np.float32)[None, :]
    r0 = p - j
    planes = [r0]
    for c in (0.0, 128.0, 256.0, 384.0):
        m = np.where(r0 + c > 0, np.float32(-1e33), np.float32(0.0))
        planes.append(r0 + m)
    cr0 = np.concatenate(planes, axis=1).astype(np.float32)
    def mplane(c):
        return np.where(r0 + c > 0, np.float32(-1e33), np.float32(0.0))
    pf = np.concatenate([r0, r0 + 128.0], axis=1)
    pd1 = np.concatenate([r0 + mplane(0.0), r0 + 128.0 + mplane(128.0)], axis=1)
    pd2 = np.concatenate([r0 + mplane(256.0), r0 + 128.0 + mplane(384.0)], axis=1)
    cr2 = np.concatenate([pf, pd1, pd2], axis=1).astype(np.float32)
    ceb = np.zeros((128, H * 16), dtype=np.float32)
    for h in range(H):
        for idx in range(16):
            coff = np.float32(128.0) * np.float32(idx - 12)
            ceb[:, h * 16 + idx] = np.float32(ALIBI_SLOPES[h]) * coff
    cones = np.ones((128, 1), dtype=BF16NP)
    conesrow = np.ones((1, 128), dtype=BF16NP)
    cmisc = np.zeros((128, NMISC), dtype=np.float32)
    cmisc[:, MC_EPS] = EPS
    cmisc[:, MC_NHALF] = -0.5
    cmisc[:, MC_QS] = QS
    return dict(cid_bf=cid_bf, cid_f=cid_f, cr0=cr0, cr2=cr2, ceb=ceb, cones=cones,
                conesrow=conesrow, cmisc=cmisc)


def _pack_u8(a_int32):
    a = np.ascontiguousarray(a_int32)
    return np.ascontiguousarray(a.view(np.uint8).reshape(a.shape[0], a.shape[1], 4)[:, :, 0])


def make_in_maps(L0_attn_x, L0_attn_Wq, L0_attn_Wk, L0_attn_Wv, L0_attn_Wo,
                 L0_ffn_gate_W_packed, L0_ffn_gate_scales, L0_ffn_up_W_packed,
                 L0_ffn_up_scales, L0_ffn_down_W_packed, L0_ffn_down_scales):
    consts = _consts()
    shared = dict(
        wq=np.ascontiguousarray(L0_attn_Wq, dtype=np.float32),
        wk=np.ascontiguousarray(L0_attn_Wk, dtype=np.float32),
        wv=np.ascontiguousarray(L0_attn_Wv, dtype=np.float32),
        wo=np.ascontiguousarray(L0_attn_Wo, dtype=np.float32),
        gpk=_pack_u8(L0_ffn_gate_W_packed),
        gsc=np.ascontiguousarray(L0_ffn_gate_scales, dtype=np.float32),
        upk=_pack_u8(L0_ffn_up_W_packed),
        usc=np.ascontiguousarray(L0_ffn_up_scales, dtype=np.float32),
        dpk=_pack_u8(L0_ffn_down_W_packed),
        dsc=np.ascontiguousarray(L0_ffn_down_scales, dtype=np.float32),
        **consts,
    )
    x = np.ascontiguousarray(L0_attn_x, dtype=np.float32).reshape(16, T, D)
    in_maps = []
    for c in range(8):
        m = dict(shared)
        m["x"] = np.ascontiguousarray(x[2 * c:2 * c + 2].reshape(TOK, D))
        in_maps.append(m)
    return in_maps


def kernel(**inputs):
    in_maps = make_in_maps(**inputs)
    nc, _ = build()
    res = run_bass_kernel_spmd(nc, in_maps, core_ids=list(range(8)))
    out = np.concatenate([r["out"].reshape(2, T, D) for r in res.results], axis=0)
    return out


# version 13
# speedup vs baseline: 1.1350x; 1.1350x over previous
import numpy as np
import ml_dtypes
from contextlib import ExitStack

import concourse.bass as bass
import concourse.mybir as mybir
from concourse.bass_utils import run_bass_kernel_spmd
from concourse.alu_op_type import AluOpType as Op

F32 = mybir.dt.float32
BF16 = mybir.dt.bfloat16
I16 = mybir.dt.int16
U8 = mybir.dt.uint8
AF = mybir.ActivationFunctionType
AX = mybir.AxisListType.X
BF16NP = ml_dtypes.bfloat16

D = 2048
TOK = 4096
T = 2048
NT = TOK // 128
H = 16
HD = 128
MLP = 16384
NNT = MLP // 128
KC = D // 128
EPS = 1e-5
QS = float(1.0 / np.sqrt(np.float32(128.0)))
GC1 = 0.7978845608028654
GC2 = 0.7978845608028654 * 0.044715

NMISC = 8
MC_EPS = 0
MC_NHALF = 1
MC_QS = 2
MC_ZERO = 3

ALIBI_SLOPES = [float(np.float32(0.5) ** (np.float32(h + 1) * np.float32(0.5))) for h in range(H)]


class Ctr:
    __slots__ = ("h", "c")
    def __init__(self, h):
        self.h = h
        self.c = 0


class Em:
    def __init__(self, which, eng, ctrs):
        self.w = which
        self.e = eng
        self.c = ctrs

    def emit(self, tag, f, incs=()):
        ins = None
        if tag == self.w:
            ins = f(self.e)
            for (s, n) in incs:
                ins = ins.then_inc(self.c[s].h, n)
        for (s, n) in incs:
            self.c[s].c += n
        return ins

    def wait(self, tag, s, val=None):
        if tag == self.w:
            self.e.wait_ge(self.c[s].h, self.c[s].c if val is None else val)

    def count(self, s):
        return self.c[s].c


def run_phase(nc, ctrs, body):
    base = {k: c.c for k, c in ctrs.items()}
    with nc.Block() as block:
        def mk(tag):
            def sec(eng):
                for k, v0 in base.items():
                    ctrs[k].c = v0
                body(Em(tag, eng, ctrs))
            sec.__name__ = f"sec_{tag}"
            return sec
        block.tensor(mk('t'))
        block.vector(mk('v'))
        block.scalar(mk('a'))
        block.sync(mk('y'))
        block.gpsimd(mk('g'))


def build(stop_after=None, debug=()):
    nc = bass.Bass("TRN2", target_bir_lowering=False)
    dbg = set(debug)

    x_h = nc.dram_tensor("x", [TOK, D], F32, kind="ExternalInput").ap()
    wq_h = nc.dram_tensor("wq", [D, D], F32, kind="ExternalInput").ap()
    wk_h = nc.dram_tensor("wk", [D, D], F32, kind="ExternalInput").ap()
    wv_h = nc.dram_tensor("wv", [D, D], F32, kind="ExternalInput").ap()
    wo_h = nc.dram_tensor("wo", [D, D], F32, kind="ExternalInput").ap()
    gpk_h = nc.dram_tensor("gpk", [MLP, D // 2], U8, kind="ExternalInput").ap()
    gsc_h = nc.dram_tensor("gsc", [MLP, D // 32], F32, kind="ExternalInput").ap()
    upk_h = nc.dram_tensor("upk", [MLP, D // 2], U8, kind="ExternalInput").ap()
    usc_h = nc.dram_tensor("usc", [MLP, D // 32], F32, kind="ExternalInput").ap()
    dpk_h = nc.dram_tensor("dpk", [D, MLP // 2], U8, kind="ExternalInput").ap()
    dsc_h = nc.dram_tensor("dsc", [D, MLP // 32], F32, kind="ExternalInput").ap()
    cid_bf_h = nc.dram_tensor("cid_bf", [128, 128], BF16, kind="ExternalInput").ap()
    cid_f_h = nc.dram_tensor("cid_f", [128, 128], F32, kind="ExternalInput").ap()
    cr3_h = nc.dram_tensor("cr3", [128, 12 * 512], BF16, kind="ExternalInput").ap()
    cdg_h = nc.dram_tensor("cdg", [128, H * 128], BF16, kind="ExternalInput").ap()
    ceb_h = nc.dram_tensor("ceb", [128, H * 16], F32, kind="ExternalInput").ap()
    cones_h = nc.dram_tensor("cones", [128, 1], BF16, kind="ExternalInput").ap()
    conesrow_h = nc.dram_tensor("conesrow", [1, 128], BF16, kind="ExternalInput").ap()
    cmisc_h = nc.dram_tensor("cmisc", [128, NMISC], F32, kind="ExternalInput").ap()

    out_h = nc.dram_tensor("out", [TOK, D], F32, kind="ExternalOutput").ap()

    sc_avT = nc.dram_tensor("sc_avT", [D, TOK], BF16, kind="Internal").ap()
    sc_x2 = nc.dram_tensor("sc_x2", [TOK, D], F32, kind="Internal").ap()
    sc_wgT = nc.dram_tensor("sc_wgT", [MLP, D], BF16, kind="Internal").ap()
    sc_wuT = nc.dram_tensor("sc_wuT", [MLP, D], BF16, kind="Internal").ap()
    sc_wdT = nc.dram_tensor("sc_wdT", [64 * 128, 4096], BF16, kind="Internal").ap()

    dbg_h = {}
    def dbg_tensor(name, shape, dtype):
        if name in dbg:
            dbg_h[name] = nc.dram_tensor("dbg_" + name, shape, dtype, kind="ExternalOutput").ap()
            return dbg_h[name]
        return None

    with ExitStack() as ctx:
        e = ctx.enter_context
        big = e(nc.sbuf_tensor("big", [128, 65536], BF16))
        cid_bf = e(nc.sbuf_tensor("sb_cid_bf", [128, 128], BF16))
        cid_f = e(nc.sbuf_tensor("sb_cid_f", [128, 128], F32))
        cones = e(nc.sbuf_tensor("sb_cones", [128, 1], BF16))
        conesrow = e(nc.sbuf_tensor("sb_conesrow", [1, 128], BF16))
        cmisc = e(nc.sbuf_tensor("sb_cmisc", [128, NMISC], F32))
        stat = e(nc.sbuf_tensor("stat", [128, 16], F32))
        sq_scr = e(nc.sbuf_tensor("sq_scr", [128, 2048], F32))
        wrow_ctx = ExitStack()
        wrow = wrow_ctx.enter_context(nc.sbuf_tensor("wrow", [128, 2 * 2048], BF16))

        sem_names = ["ly", "lyA", "lyB", "lw", "lx", "lg", "st", "stA", "stB",
                     "spA", "spB", "fwA", "fwB", "dwA", "dwB", "dwC", "wtr", "vtr",
                     "t0", "t1", "v0", "a0"]
        ctrs = {n: Ctr(e(nc.semaphore(n))) for n in sem_names}

        meta = {"ctrs": ctrs}

        def ph_consts(em):
            em.emit('y', lambda y: y.dma_start(cid_bf[:], cid_bf_h), [("ly", 16)])
            em.emit('y', lambda y: y.dma_start(cid_f[:], cid_f_h), [("ly", 16)])
            em.emit('y', lambda y: y.dma_start(cones[:], cones_h), [("ly", 16)])
            em.emit('y', lambda y: y.dma_start(conesrow[:], conesrow_h), [("ly", 16)])
            em.emit('y', lambda y: y.dma_start(cmisc[:], cmisc_h), [("ly", 16)])
            em.wait('y', "ly")
        run_phase(nc, ctrs, ph_consts)

        def fence(em):
            for tg in ('t', 'v', 'a', 'y', 'g'):
                for sm in ("st", "stA", "stB", "spA", "spB"):
                    em.wait(tg, sm)

        lctx = ExitStack()
        pst = lctx.enter_context(nc.psum_tensor("pst_l", [128, 2048], BF16))
        xin = lctx.enter_context(nc.sbuf_tensor("xin", [128, 2 * 2048], F32))
        ln1o = lctx.enter_context(nc.sbuf_tensor("ln1o", [128, 2 * 2048], BF16))

        def ln_stats(em, src_ap, rstd_col, nmean_col, n_elems, sq_slice, sqcol=5):
            em.emit('v', lambda v: v.reduce_sum(stat[:, 0:1], src_ap, AX), [("v0", 1)])
            em.emit('a', lambda a: a.activation(sq_slice, src_ap, AF.Square,
                                                accum_out=stat[:, sqcol:sqcol + 1]), [("a0", 1)])
            em.wait('v', "v0")
            em.emit('v', lambda v: v.tensor_scalar_mul(stat[:, nmean_col:nmean_col + 1],
                                                       stat[:, 0:1], -1.0 / n_elems), [("v0", 1)])
            em.wait('v', "v0")
            em.emit('v', lambda v: v.tensor_mul(stat[:, 2:3], stat[:, nmean_col:nmean_col + 1],
                                                stat[:, nmean_col:nmean_col + 1]), [("v0", 1)])
            em.wait('v', "a0")
            em.wait('v', "v0")
            em.emit('v', lambda v: v.scalar_tensor_tensor(stat[:, 3:4], stat[:, sqcol:sqcol + 1],
                                                          1.0 / n_elems, stat[:, 2:3],
                                                          Op.mult, Op.subtract), [("v0", 1)])
            em.wait('a', "v0")
            em.emit('a', lambda a: a.activation(stat[:, 6:7], stat[:, 3:4], AF.Ln,
                                                bias=cmisc[:, MC_EPS:MC_EPS + 1]), [("a0", 1)])
            em.wait('a', "a0")
            em.emit('a', lambda a: a.activation(stat[:, rstd_col:rstd_col + 1], stat[:, 6:7], AF.Exp,
                                                scale=cmisc[:, MC_NHALF:MC_NHALF + 1]), [("a0", 1)])

        def ph_ln1(em):
            for i in range(NT):
                b = i % 2
                xv = xin[:, b * 2048:(b + 1) * 2048]
                lv = ln1o[:, b * 2048:(b + 1) * 2048]
                if i >= 2:
                    em.wait('y', "t0", (i - 2 + 1) * KC)
                lsem = "lyA" if i % 2 == 0 else "lyB"
                em.emit('y', lambda y, i=i, xv=xv: y.dma_start(xv, x_h[i * 128:(i + 1) * 128, :]),
                        [(lsem, 16)])
                em.wait('v', lsem)
                em.wait('a', lsem)
                ln_stats(em, xv, rstd_col=4, nmean_col=1, n_elems=D, sq_slice=sq_scr[:])
                em.wait('v', "a0")
                em.emit('v', lambda v, xv=xv, lv=lv: v.tensor_scalar(
                    lv, xv, stat[:, 1:2], stat[:, 4:5], Op.add, Op.mult), [("v0", 1)])
                em.wait('t', "v0")
                for c in range(KC):
                    pb = c % 2
                    a0_before = em.count("a0")
                    if a0_before > 0:
                        em.wait('t', "a0", max(0, a0_before - 1))
                    em.emit('t', lambda t, lv=lv, c=c, pb=pb: t.transpose(
                        pst[:, pb * 1024:pb * 1024 + 128], lv[:, c * 128:(c + 1) * 128], cid_bf[:]),
                        [("t0", 1)])
                    em.wait('a', "t0")
                    em.emit('a', lambda a, c=c, i=i, pb=pb: a.copy(
                        big[:, c * 4096 + i * 128: c * 4096 + (i + 1) * 128],
                        pst[:, pb * 1024:pb * 1024 + 128]), [("a0", 1)])
        run_phase(nc, ctrs, ph_ln1)
        lctx.close()

        if "ln1xt" in dbg:
            tap = dbg_tensor("ln1xt", [128, 65536], BF16)
            def ph_dbg1(em):
                em.emit('y', lambda y: y.dma_start(tap, big[:]), [("st", 16)])
                em.wait('y', "st")
            run_phase(nc, ctrs, ph_dbg1)

        if stop_after == "ln1":
            _finish(nc, ctrs, out_h, big)
            return nc, dbg_h

        with ExitStack() as actx:
            ae = actx.enter_context
            ceb = ae(nc.sbuf_tensor("sb_ceb", [128, H * 16], F32))
            wT = ae(nc.sbuf_tensor("wT", [128, 3 * 2048], BF16))
            qT = ae(nc.sbuf_tensor("qT", [128, TOK], BF16))
            kT = ae(nc.sbuf_tensor("kT", [128, TOK], BF16))
            vstage = ae(nc.sbuf_tensor("vstage", [128, 512], BF16))
            vtok = ae(nc.sbuf_tensor("vtok", [128, TOK], BF16))
            ptile = ae(nc.sbuf_tensor("ptile", [128, 4 * 512], BF16))
            recbf = ae(nc.sbuf_tensor("recbf", [1, 2 * 512], BF16))
            plane = ae(nc.sbuf_tensor("plane", [128, 2 * 512], BF16))
            avsb = ae(nc.sbuf_tensor("avsb", [128, 2 * 512], BF16))
            cr3 = ae(nc.sbuf_tensor("sb_cr3", [128, 12 * 512], BF16))
            cdg = ae(nc.sbuf_tensor("sb_cdg", [128, 2 * 128], BF16))
            ps_a = ae(nc.psum_tensor("ps_a", [128, 1024], F32))
            ps_b = ae(nc.psum_tensor("ps_b", [128, 1024], F32))
            ps_av = ae(nc.psum_tensor("ps_av", [128, 1024], F32))
            ps_sum = ae(nc.psum_tensor("ps_sum", [128, 1024], F32))

            def ph_attn_consts(em):
                fence(em)
                em.emit('y', lambda y: y.dma_start(cr3[:], cr3_h), [("ly", 16)])
                em.emit('y', lambda y: y.dma_start(ceb[:], ceb_h), [("ly", 16)])
                em.wait('y', "ly")
            run_phase(nc, ctrs, ph_attn_consts)

            w_hs = {"q": wq_h, "k": wk_h, "v": wv_h}

            def ph_head(em, h, trk):
                sbank = [ps_a, ps_b]
                hp = h % 2
                em.wait('y', "t0", trk.get(("cdgdone", hp), 0))
                em.emit('y', lambda y, h=h, hp=hp: y.dma_start(
                    cdg[:, hp * 128:(hp + 1) * 128], cdg_h[:, h * 128:(h + 1) * 128]), [("lw", 16)])
                em.wait('t', "a0", trk.get("last_exp", 0))
                for pi in range(3):
                    pn = ("q", "k", "v")[pi]
                    gpar = (h * 3 + pi) % 2
                    em.wait('g', "wtr")
                    em.emit('g', lambda g, pn=pn, h=h, gpar=gpar: g.dma_start(
                        wrow[:, gpar * 2048:(gpar + 1) * 2048],
                        w_hs[pn][h * 128:(h + 1) * 128, :]), [("lg", 16)])
                    em.wait('y', "lg")
                    em.wait('y', "t0", trk.get(("qkv_t0", pi), 0))
                    for c in range(KC):
                        em.emit('y', lambda y, pi=pi, c=c, gpar=gpar: y.dma_start(
                            wT[:, pi * 2048 + c * 128: pi * 2048 + (c + 1) * 128],
                            wrow[:, gpar * 2048 + c * 128: gpar * 2048 + (c + 1) * 128],
                            transpose=True), [("wtr", 16)])
                em.wait('t', "wtr")
                for pi, pn in enumerate(("q", "k", "v")):
                    for tc in range(8):
                        bb = tc % 2
                        a0b = em.count("a0")
                        em.wait('t', "a0", max(0, a0b - 1))
                        for c in range(KC):
                            em.emit('t', lambda t, pi=pi, tc=tc, c=c, bb=bb: t.matmul(
                                ps_a[:, bb * 512:bb * 512 + 512],
                                wT[:, pi * 2048 + c * 128: pi * 2048 + (c + 1) * 128],
                                big[:, c * 4096 + tc * 512:(c * 4096 + (tc + 1) * 512)],
                                start=(c == 0), stop=(c == KC - 1)),
                                [("t0", 1)] if c == KC - 1 else [])
                        em.wait('a', "t0")
                        if pn == "q":
                            em.emit('a', lambda a, tc=tc, bb=bb: a.activation(
                                qT[:, tc * 512:(tc + 1) * 512], ps_a[:, bb * 512:bb * 512 + 512],
                                AF.Identity, scale=cmisc[:, MC_QS:MC_QS + 1]), [("a0", 1)])
                        elif pn == "k":
                            em.emit('a', lambda a, tc=tc, bb=bb: a.copy(
                                kT[:, tc * 512:(tc + 1) * 512], ps_a[:, bb * 512:bb * 512 + 512]),
                                [("a0", 1)])
                        else:
                            em.wait('a', "vtr")
                            em.emit('a', lambda a, tc=tc, bb=bb: a.copy(
                                vstage[:], ps_a[:, bb * 512:bb * 512 + 512]), [("a0", 1)])
                            em.wait('y', "a0")
                            for u in range(4):
                                em.emit('y', lambda y, tc=tc, u=u: y.dma_start(
                                    vtok[:, tc * 512 + u * 128: tc * 512 + (u + 1) * 128],
                                    vstage[:, u * 128:(u + 1) * 128], transpose=True),
                                    [("vtr", 16)])
                    if pn == "v":
                        pass
                    trk[("qkv_t0", pi)] = em.count("t0")
                em.wait('t', "a0")
                em.wait('t', "vtr")
                em.wait('t', "lw")
                batches = []
                for g_ in range(4):
                    nb = 2 * (g_ + 1)
                    for b_ in range(nb):
                        for s_ in range(2):
                            batches.append((s_, g_, 2 * b_, b_ == nb - 1))
                exp_pos = {}
                av_pos = {}
                for bi, (s_, g_, kt0, lastb) in enumerate(batches):
                    pb2 = bi % 2
                    sb_ = sbank[pb2]
                    didx = kt0 - 4 * g_ + 12
                    if bi >= 2:
                        em.wait('t', "a0", exp_pos[bi - 2])
                    if kt0 == 0:
                        em.wait('t', "v0", trk.get(("plane_s", s_), 0))
                        em.wait('t', "v0", trk.get(("norm_s", s_), 0))
                    for half in range(2):
                        kt = kt0 + half
                        if kt < 4 * g_:
                            pcol = half * 512
                        else:
                            pcol = (2 + (kt - 4 * g_)) * 512
                        em.emit('t', lambda t, s_=s_, g_=g_, kt=kt, sb_=sb_, half=half: t.matmul(
                            sb_[:, half * 512:(half + 1) * 512],
                            kT[:, s_ * 2048 + kt * 128: s_ * 2048 + (kt + 1) * 128],
                            qT[:, s_ * 2048 + g_ * 512: s_ * 2048 + (g_ + 1) * 512],
                            start=True, stop=False), [])
                        em.emit('t', lambda t, sb_=sb_, half=half, h=h, pcol=pcol: t.matmul(
                            sb_[:, half * 512:(half + 1) * 512],
                            cdg[:, hp * 128:(hp + 1) * 128],
                            cr3[:, pcol:pcol + 512],
                            start=False, stop=False), [])
                        em.emit('t', lambda t, sb_=sb_, half=half, h=h, pcol=pcol: t.matmul(
                            sb_[:, half * 512:(half + 1) * 512],
                            cdg[:, hp * 128:(hp + 1) * 128],
                            cr3[:, 3072 + pcol:3072 + pcol + 512],
                            start=False, stop=True), [("t0", 1)] if half == 1 else [])
                    mmp = em.count("t0")
                    em.wait('a', "t0", mmp)
                    if bi >= 2:
                        em.wait('a', "t0", av_pos[bi - 2])
                    em.emit('a', lambda a, pb2=pb2, h=h, didx=didx, sb_=sb_: a.activation(
                        ptile[:, pb2 * 1024:(pb2 + 1) * 1024], sb_[:],
                        AF.Exp, bias=ceb[:, h * 16 + didx: h * 16 + didx + 1]), [("a0", 1)])
                    exp_pos[bi] = em.count("a0")
                    em.wait('t', "a0", exp_pos[bi])
                    nkt = 4 * (g_ + 1)
                    for half in range(2):
                        kt = kt0 + half
                        em.emit('t', lambda t, kt=kt, pb2=pb2, half=half, s_=s_, nkt=nkt: t.matmul(
                            ps_sum[0:1, s_ * 512:s_ * 512 + 512], cones[:],
                            ptile[:, pb2 * 1024 + half * 512: pb2 * 1024 + (half + 1) * 512],
                            start=(kt == 0), stop=(kt == nkt - 1)), [])
                        em.emit('t', lambda t, s_=s_, kt=kt, pb2=pb2, half=half, nkt=nkt: t.matmul(
                            ps_av[:, s_ * 512:s_ * 512 + 512],
                            vtok[:, (s_ * 16 + kt) * 128:(s_ * 16 + kt + 1) * 128],
                            ptile[:, pb2 * 1024 + half * 512: pb2 * 1024 + (half + 1) * 512],
                            start=(kt == 0), stop=(kt == nkt - 1)),
                            [("t0", 1)] if half == 1 else [])
                    av_pos[bi] = em.count("t0")
                    if lastb:
                        pp = s_
                        em.wait('v', "t0", av_pos[bi])
                        def _recip(v, pp=pp):
                            with nc.allow_low_precision(reason="softmax 1/S bf16; LN scale-inv"):
                                return v.reciprocal(recbf[:, pp * 512:(pp + 1) * 512],
                                                    ps_sum[0:1, pp * 512:pp * 512 + 512])
                        em.emit('v', _recip, [("v0", 1)])
                        rp = em.count("v0")
                        em.wait('t', "v0", rp)
                        em.emit('t', lambda t, pp=pp: t.matmul(
                            ps_sum[:, pp * 512:pp * 512 + 512], conesrow[:],
                            recbf[:, pp * 512:(pp + 1) * 512], start=True, stop=True), [("t0", 1)])
                        plp = em.count("t0")
                        em.wait('v', "t0", plp)
                        em.emit('v', lambda v, pp=pp: v.tensor_copy(
                            plane[:, pp * 512:(pp + 1) * 512], ps_sum[:, pp * 512:pp * 512 + 512]),
                            [("v0", 1)])
                        trk[("plane_s", s_)] = em.count("v0")
                        em.wait('v', "v0", trk[("plane_s", s_)])
                        em.wait('v', "stA" if pp == 0 else "stB")
                        em.emit('v', lambda v, pp=pp: v.tensor_mul(
                            avsb[:, pp * 512:(pp + 1) * 512],
                            ps_av[:, pp * 512:pp * 512 + 512],
                            plane[:, pp * 512:(pp + 1) * 512]), [("v0", 1)])
                        trk[("norm_s", s_)] = em.count("v0")
                        em.wait('y', "v0", trk[("norm_s", s_)])
                        em.emit('y', lambda y, h=h, s_=s_, g_=g_, pp=pp: y.dma_start(
                            sc_avT[h * 128:(h + 1) * 128,
                                   s_ * 2048 + g_ * 512: s_ * 2048 + (g_ + 1) * 512],
                            avsb[:, pp * 512:(pp + 1) * 512]),
                            [("stA" if pp == 0 else "stB", 16)])
                trk["last_exp"] = exp_pos[len(batches) - 1]
                trk[("cdgdone", hp)] = em.count("t0")

            def ph_attention(em):
                trk = {}
                for h in range(H):
                    ph_head(em, h, trk)
            run_phase(nc, ctrs, ph_attention)

        if "avt" in dbg:
            tap = dbg_tensor("avt", [D, TOK], BF16)
            def ph_dbga(em):
                for r in range(4):
                    em.emit('y', lambda y, r=r: y.dma_start(
                        tap[r * 512:(r + 1) * 512, :], sc_avT[r * 512:(r + 1) * 512, :]),
                        [("st", 16)])
                em.wait('y', "st")
            run_phase(nc, ctrs, ph_dbga)

        if stop_after == "attn":
            _finish(nc, ctrs, out_h, big)
            return nc, dbg_h

        bigf = big[:].bitcast(F32)
        with ExitStack() as wctx:
            we = wctx.enter_context
            projst = we(nc.sbuf_tensor("projst", [128, 2 * 512], F32))
            x2t = we(nc.sbuf_tensor("x2t", [128, 2048], F32))
            xres = we(nc.sbuf_tensor("xres", [128, 2048], F32))
            tmpn = we(nc.sbuf_tensor("tmpn", [128, 2048], F32))
            ps_p = we(nc.psum_tensor("ps_p", [128, 1024], F32))
            ps_tr = we(nc.psum_tensor("ps_tr", [128, 1024], F32))

            def ph_wo_prep(em):
                fence(em)
                for m in range(16):
                    gpar = m % 2
                    em.wait('g', "wtr")
                    em.emit('g', lambda g, m=m, gpar=gpar: g.dma_start(
                        wrow[:, gpar * 2048:(gpar + 1) * 2048],
                        wo_h[m * 128:(m + 1) * 128, :]), [("lg", 16)])
                    em.wait('y', "lg")
                    for c in range(KC):
                        em.emit('y', lambda y, c=c, m=m, gpar=gpar: y.dma_start(
                            big[:, 8192 + c * 2048 + m * 128: 8192 + c * 2048 + (m + 1) * 128],
                            wrow[:, gpar * 2048 + c * 128: gpar * 2048 + (c + 1) * 128],
                            transpose=True), [("wtr", 16)])
                em.wait('t', "wtr")
                em.wait('v', "wtr")
            run_phase(nc, ctrs, ph_wo_prep)

            def ph_wo_tc(em, tc):
                em.wait('y', "stA")
                em.wait('y', "stB")
                em.emit('y', lambda y, tc=tc: y.dma_start(
                    big[:, 0:8192],
                    sc_avT.rearrange("(c p) t -> p c t", p=128)[:, :, tc * 512:(tc + 1) * 512]),
                    [("lw", 16)])
                em.wait('t', "lw")
                for m in range(16):
                    bb = m % 2
                    a0b = em.count("a0")
                    em.wait('t', "a0", max(0, a0b - 1))
                    for dch in range(16):
                        em.emit('t', lambda t, m=m, dch=dch, bb=bb: t.matmul(
                            ps_p[:, bb * 512:bb * 512 + 512],
                            big[:, 8192 + dch * 2048 + m * 128: 8192 + dch * 2048 + (m + 1) * 128],
                            big[:, dch * 512:(dch + 1) * 512],
                            start=(dch == 0), stop=(dch == KC - 1)),
                            [("t0", 1)] if dch == KC - 1 else [])
                    em.wait('a', "t0")
                    em.emit('a', lambda a, m=m, bb=bb: a.copy(
                        projst[:, bb * 512:(bb + 1) * 512], ps_p[:, bb * 512:bb * 512 + 512]),
                        [("a0", 1)])
                    for u in range(4):
                        pb = u % 2
                        v0b = em.count("v0")
                        em.wait('t', "v0", max(0, v0b - 1))
                        em.wait('t', "a0")
                        em.emit('t', lambda t, u=u, pb=pb, bb=bb: t.transpose(
                            ps_tr[:, pb * 512:pb * 512 + 128],
                            projst[:, bb * 512 + u * 128: bb * 512 + (u + 1) * 128], cid_f[:]),
                            [("t0", 1)])
                        em.wait('v', "t0")
                        em.emit('v', lambda v, u=u, m=m, pb=pb: v.tensor_copy(
                            bigf[:, 20480 + u * 2048 + m * 128: 20480 + u * 2048 + (m + 1) * 128],
                            ps_tr[:, pb * 512:pb * 512 + 128]), [("v0", 1)])
                for u in range(4):
                    row0 = tc * 512 + u * 128
                    xo = bigf[:, 20480 + u * 2048: 20480 + (u + 1) * 2048]
                    em.wait('y', "v0")
                    em.emit('y', lambda y, row0=row0: y.dma_start(
                        xres[:], x_h[row0:row0 + 128, :]), [("lx", 16)])
                    em.wait('v', "v0")
                    em.wait('a', "v0")
                    ln_stats(em, xo, rstd_col=4, nmean_col=1, n_elems=D, sq_slice=sq_scr[:])
                    em.wait('v', "a0")
                    em.emit('v', lambda v, xo=xo: v.tensor_scalar(
                        tmpn[:], xo, stat[:, 1:2], stat[:, 4:5], Op.add, Op.mult), [("v0", 1)])
                    em.wait('v', "lx")
                    em.wait('v', "st")
                    em.wait('v', "v0")
                    em.emit('v', lambda v: v.tensor_add(x2t[:], tmpn[:], xres[:]), [("v0", 1)])
                    em.wait('y', "v0")
                    em.emit('y', lambda y, row0=row0: y.dma_start(
                        sc_x2[row0:row0 + 128, :], x2t[:]), [("st", 16)])

            for tc in range(8):
                run_phase(nc, ctrs, lambda em, tc=tc: ph_wo_tc(em, tc))

        if "x2" in dbg:
            tap = dbg_tensor("x2", [TOK, D], F32)
            def ph_dbgx2(em):
                for r in range(8):
                    em.emit('y', lambda y, r=r: y.dma_start(
                        tap[r * 512:(r + 1) * 512, :], sc_x2[r * 512:(r + 1) * 512, :]),
                        [("st", 16)])
                em.wait('y', "st")
            run_phase(nc, ctrs, ph_dbgx2)

        if stop_after == "wo":
            _finish(nc, ctrs, out_h, big)
            return nc, dbg_h

        wrow_ctx.close()

        bigI = big[:].bitcast(I16)
        q_pkb = [bigI[:, 0:2048], bigI[:, 2048:4096]]
        q_gb = bigI[:, 8192:12288]
        q_sb = bigI[:, 12288:16384]
        q_mg2 = big[:, 16384:20480]
        q_mg = big[:, 20480:24576]
        q_ws = big[:, 24576:28672]
        q_wsc2 = [big[:, 28672:32768], big[:, 45056:49152]]
        q_wt = big[:, 32768:40960]
        q_sx = big[:, 40960:45056]
        a0_done = {}

        with ExitStack() as qctx:
            qe = qctx.enter_context
            pst = qe(nc.psum_tensor("pst_q", [128, 2048], BF16))
            scl = qe(nc.sbuf_tensor("scl", [128, 1024], F32))
            scl4 = qe(nc.sbuf_tensor("scl4", [128, 1024], F32))

            class Chain:
                __slots__ = ("ops", "pos", "i")
                def __init__(self):
                    self.ops = []
                    self.pos = None
                    self.i = 0

            def decode_chain(pkb_ap, which_half, gb, sb, mg2, mg, ws, sx_ap, out_ap):
                ch = Chain()
                if which_half == 0:
                    ch.ops.append(lambda v, a=pkb_ap, o=gb: v.tensor_scalar(
                        o, a, 6, 0x3C0, Op.logical_shift_left, Op.bitwise_and))
                else:
                    ch.ops.append(lambda v, a=pkb_ap, o=gb: v.tensor_scalar(
                        o, a, 4, 6, Op.logical_shift_right, Op.logical_shift_left))
                ch.ops.append(lambda v, i=gb, o=sb: v.tensor_scalar(
                    o, i, 0x200, 6, Op.bitwise_and, Op.logical_shift_left))
                ch.ops.append(lambda v, i=gb: v.tensor_scalar(
                    i, i, 0x1C0, 0x3E00, Op.bitwise_and, Op.bitwise_or))
                ch.ops.append(lambda v, i=gb, o=mg2: v.tensor_scalar(
                    o, i.bitcast(BF16), 2.0, -0.25, Op.mult, Op.add))
                ch.ops.append(lambda v, i=gb, m2=mg2, o=mg: v.tensor_tensor(
                    o, i.bitcast(BF16), m2, Op.min))
                ch.ops.append(lambda v, m=mg, sb_=sb, o=ws: v.tensor_tensor(
                    o.bitcast(I16), m.bitcast(I16), sb_, Op.bitwise_or))
                ch.ops.append(lambda v, w=ws, sx=sx_ap, o=out_ap: v.tensor_tensor(
                    o, w, sx, Op.mult))
                return ch

            def run_chains(em, chains):
                live = True
                while live:
                    live = False
                    for ch in chains:
                        if ch.i < len(ch.ops):
                            live = True
                            if ch.pos is not None:
                                em.wait('v', "v0", ch.pos)
                            em.emit('v', ch.ops[ch.i], [("v0", 1)])
                            ch.pos = em.count("v0")
                            ch.i += 1

            v0_done = {}
            a0_grp = []

            def emit_groups(em, n_groups, src_par, wt_off, spsem):
                for grp in range(n_groups):
                    pb = grp % 2
                    if len(a0_grp) >= 2:
                        em.wait('t', "a0", a0_grp[-2])
                    for i in range(8):
                        em.emit('t', lambda t, grp=grp, i=i, pb=pb, sp_=src_par: t.transpose(
                            pst[:, pb * 1024 + i * 128: pb * 1024 + (i + 1) * 128],
                            q_wsc2[sp_][:, grp * 1024 + i * 128: grp * 1024 + (i + 1) * 128],
                            cid_bf[:]), [("t0", 1)] if i == 7 else [])
                    em.wait('a', "t0")
                    em.wait('a', spsem)
                    em.emit('a', lambda a, grp=grp, pb=pb, wo=wt_off: a.copy(
                        q_wt[:, wo + grp * 1024: wo + (grp + 1) * 1024],
                        pst[:, pb * 1024: (pb + 1) * 1024]), [("a0", 1)])
                    a0_grp.append(em.count("a0"))

            def gu_loads(em, nt):
                par = nt % 2
                lsem = "lyA" if par == 0 else "lyB"
                ssem = "lw" if par == 0 else "lx"
                if nt >= 1:
                    em.wait('g', "v0", v0_done.get(("gu", nt - 1), 0))
                    em.wait('y', "v0", v0_done.get(("gu", nt - 1), 0))
                em.emit('g', lambda g, nt=nt, par=par: g.dma_start(
                    q_pkb[par][:, 0:1024], gpk_h[nt * 128:(nt + 1) * 128, :]), [(lsem, 16)])
                em.emit('g', lambda g, nt=nt, par=par: g.dma_start(
                    q_pkb[par][:, 1024:2048], upk_h[nt * 128:(nt + 1) * 128, :]), [(lsem, 16)])
                em.emit('y', lambda y, nt=nt, par=par: y.dma_start(
                    scl[:, par * 128: par * 128 + 64], gsc_h[nt * 128:(nt + 1) * 128, :]),
                    [(ssem, 16)])
                em.emit('y', lambda y, nt=nt, par=par: y.dma_start(
                    scl[:, par * 128 + 64: par * 128 + 128], usc_h[nt * 128:(nt + 1) * 128, :]),
                    [(ssem, 16)])

            def ph_dequant_gu(em, nt):
                par = nt % 2
                lsem = "lyA" if par == 0 else "lyB"
                ssem = "lw" if par == 0 else "lx"
                if nt + 1 < NNT:
                    gu_loads(em, nt + 1)
                em.wait('v', lsem)
                em.wait('v', ssem)
                em.emit('v', lambda v, par=par: v.tensor_scalar_mul(
                    scl4[:, par * 128:par * 128 + 128], scl[:, par * 128:par * 128 + 128], 4.0),
                    [("v0", 1)])
                em.wait('v', "v0")
                for ti in range(2):
                    sc_src = scl4[:, par * 128 + ti * 64: par * 128 + (ti + 1) * 64]
                    em.emit('v', lambda v, ti=ti, sc_src=sc_src: v.tensor_copy(
                        q_sx[:, ti * 2048: ti * 2048 + 1024].rearrange("p (n o) -> p n o", o=16),
                        sc_src.rearrange("p (n o) -> p n o", o=1).broadcast_to([128, 64, 16])),
                        [("v0", 1)])
                em.wait('v', "v0")
                em.wait('v', "a0", a0_done.get(("gu", nt - 2), 0))
                q_wsc = q_wsc2[par]
                chains = []
                for ti in range(2):
                    for hf in range(2):
                        q = ti * 2 + hf
                        chains.append(decode_chain(
                            q_pkb[par][:, ti * 1024:(ti + 1) * 1024], hf,
                            q_gb[:, q * 1024:(q + 1) * 1024], q_sb[:, q * 1024:(q + 1) * 1024],
                            q_mg2[:, q * 1024:(q + 1) * 1024],
                            q_mg[:, q * 1024:(q + 1) * 1024], q_ws[:, q * 1024:(q + 1) * 1024],
                            q_sx[:, ti * 2048: ti * 2048 + 1024],
                            q_wsc[:, (ti * 2 + hf) * 1024:(ti * 2 + hf + 1) * 1024]))
                run_chains(em, chains)
                v0_done[("gu", nt)] = em.count("v0")
                em.wait('t', "v0")
                wt_off = par * 4096
                spsem = "spA" if par == 0 else "spB"
                for ti in range(2):
                    for grp in range(2):
                        pb = (ti * 2 + grp) % 2
                        if len(a0_grp) >= 2:
                            em.wait('t', "a0", a0_grp[-2])
                        for i in range(8):
                            cc = grp * 8 + i
                            c, hf = cc // 2, cc % 2
                            em.emit('t', lambda t, q_wsc=q_wsc, ti=ti, c=c, hf=hf, i=i, pb=pb: t.transpose(
                                pst[:, pb * 1024 + i * 128: pb * 1024 + (i + 1) * 128],
                                q_wsc[:, ti * 2048 + hf * 1024 + c * 128:
                                      ti * 2048 + hf * 1024 + (c + 1) * 128],
                                cid_bf[:]), [("t0", 1)] if i == 7 else [])
                        em.wait('a', "t0")
                        em.wait('a', spsem)
                        em.emit('a', lambda a, ti=ti, grp=grp, pb=pb, wo=wt_off: a.copy(
                            q_wt[:, wo + ti * 2048 + grp * 1024: wo + ti * 2048 + (grp + 1) * 1024],
                            pst[:, pb * 1024:(pb + 1) * 1024]), [("a0", 1)])
                        a0_grp.append(em.count("a0"))
                a0_done[("gu", nt)] = em.count("a0")
                em.wait('y', "a0")
                em.emit('y', lambda y, nt=nt, wo=wt_off: y.dma_start(
                    sc_wgT[nt * 128:(nt + 1) * 128, :], q_wt[:, wo:wo + 2048]), [(spsem, 16)])
                em.emit('y', lambda y, nt=nt, wo=wt_off: y.dma_start(
                    sc_wuT[nt * 128:(nt + 1) * 128, :], q_wt[:, wo + 2048:wo + 4096]), [(spsem, 16)])

            def wd_loads(em, j, sl):
                gi = j * 4 + sl
                par = gi % 2
                lsem = "lyA" if par == 0 else "lyB"
                if gi >= 1:
                    em.wait('g', "v0", v0_done.get(("wd", gi - 1), v0_done.get(("gu", NNT - 1), 0)))
                em.emit('g', lambda g, j=j, sl=sl, par=par: g.dma_start(
                    q_pkb[par][:], dpk_h[j * 128:(j + 1) * 128, sl * 2048:(sl + 1) * 2048]),
                    [(lsem, 16)])
                if sl == 0:
                    ssem = "lw" if (j % 2) == 0 else "lx"
                    em.wait('y', "v0", v0_done.get(("wdscl", j - 1),
                                                   v0_done.get(("gu", NNT - 1), 0)))
                    em.emit('y', lambda y, j=j: y.dma_start(
                        scl[:, (j % 2) * 512:(j % 2) * 512 + 512],
                        dsc_h[j * 128:(j + 1) * 128, :]), [(ssem, 16)])

            def ph_dequant_wd(em, j, sl):
                gi = j * 4 + sl
                par = gi % 2
                lsem = "lyA" if par == 0 else "lyB"
                if gi + 1 < H * 4:
                    wd_loads(em, (gi + 1) // 4, (gi + 1) % 4)
                if sl == 0:
                    ssem = "lw" if (j % 2) == 0 else "lx"
                    em.wait('v', ssem)
                    em.emit('v', lambda v, j=j: v.tensor_scalar_mul(
                        scl4[:, (j % 2) * 512:(j % 2) * 512 + 512],
                        scl[:, (j % 2) * 512:(j % 2) * 512 + 512], 4.0), [("v0", 1)])
                    v0_done[("wdscl", j)] = em.count("v0")
                    em.wait('v', "v0")
                em.wait('v', lsem)
                sc_src = scl4[:, (j % 2) * 512 + sl * 128:(j % 2) * 512 + (sl + 1) * 128]
                em.emit('v', lambda v, sc_src=sc_src: v.tensor_copy(
                    q_sx[:, 0:2048].rearrange("p (n o) -> p n o", o=16),
                    sc_src.rearrange("p (n o) -> p n o", o=1).broadcast_to([128, 128, 16])),
                    [("v0", 1)])
                em.wait('v', "v0")
                em.wait('v', "a0", a0_done.get(("wd", gi - 2), a0_done.get(("gu", NNT - 1), 0)))
                q_wsc = q_wsc2[par]
                ev = q_wsc.rearrange("p (j two) -> p j two", two=2)[:, :, 0]
                od = q_wsc.rearrange("p (j two) -> p j two", two=2)[:, :, 1]
                chains = []
                for hf, out_ap in ((0, ev), (1, od)):
                    chains.append(decode_chain(
                        q_pkb[par][:], hf,
                        q_gb[:, hf * 2048:(hf + 1) * 2048], q_sb[:, hf * 2048:(hf + 1) * 2048],
                        q_mg2[:, hf * 2048:(hf + 1) * 2048], q_mg[:, hf * 2048:(hf + 1) * 2048],
                        q_ws[:, hf * 2048:(hf + 1) * 2048], q_sx[:, 0:2048], out_ap))
                run_chains(em, chains)
                v0_done[("wd", gi)] = em.count("v0")
                em.wait('t', "v0")
                wt_off = par * 4096
                spsem = "spA" if par == 0 else "spB"
                emit_groups(em, 4, par, wt_off, spsem)
                a0_done[("wd", gi)] = em.count("a0")
                em.wait('y', "a0")
                base = gi * 128
                em.emit('y', lambda y, base=base, wo=wt_off: y.dma_start(
                    sc_wdT[base:base + 128, :], q_wt[:, wo:wo + 4096]), [(spsem, 16)])

            def ph_dequant(em):
                fence(em)
                gu_loads(em, 0)
                for nt in range(NNT):
                    ph_dequant_gu(em, nt)
                wd_loads(em, 0, 0)
                for j in range(H):
                    for sl in range(4):
                        ph_dequant_wd(em, j, sl)
            run_phase(nc, ctrs, ph_dequant)

        if "wgt" in dbg:
            tap = dbg_tensor("wgt", [MLP, D], BF16)
            def ph_dbgw(em):
                for r in range(16):
                    em.emit('y', lambda y, r=r: y.dma_start(
                        tap[r * 1024:(r + 1) * 1024, :], sc_wgT[r * 1024:(r + 1) * 1024, :]),
                        [("st", 16)])
                em.wait('y', "st")
            run_phase(nc, ctrs, ph_dbgw)
        if "wdt" in dbg:
            tap = dbg_tensor("wdt", [8192, 4096], BF16)
            def ph_dbgwd(em):
                for r in range(16):
                    em.emit('y', lambda y, r=r: y.dma_start(
                        tap[r * 512:(r + 1) * 512, :], sc_wdT[r * 512:(r + 1) * 512, :]),
                        [("st", 16)])
                em.wait('y', "st")
            run_phase(nc, ctrs, ph_dbgwd)

        if stop_after == "dequant":
            _finish(nc, ctrs, out_h, big)
            return nc, dbg_h

        ln3xT = e(nc.sbuf_tensor("ln3xT", [128, 16 * 512], BF16))
        v0_hist = {}
        a0_hist = {}
        for tc in range(8):
            with ExitStack() as fa:
                fe = fa.enter_context
                pst = fe(nc.psum_tensor(f"pst_f_{tc}", [128, 2048], BF16))
                x2in = fe(nc.sbuf_tensor(f"x2in_{tc}", [128, 2 * 2048], F32))
                ln3o = fe(nc.sbuf_tensor(f"ln3o_{tc}", [128, 2 * 2048], BF16))

                def ph_ln3(em, tc=tc):
                    fence(em)
                    for u in range(4):
                        b = u % 2
                        lsem = "lyA" if b == 0 else "lyB"
                        xv = x2in[:, b * 2048:(b + 1) * 2048]
                        lv = ln3o[:, b * 2048:(b + 1) * 2048]
                        em.wait('y', "v0")
                        row0 = tc * 512 + u * 128
                        em.emit('y', lambda y, row0=row0, xv=xv: y.dma_start(
                            xv, sc_x2[row0:row0 + 128, :]), [(lsem, 16)])
                        em.wait('v', lsem)
                        em.wait('a', lsem)
                        ln_stats(em, xv, rstd_col=4, nmean_col=1, n_elems=D, sq_slice=sq_scr[:],
                                 sqcol=5 + (u % 2) * 2)
                        em.wait('v', "a0")
                        em.wait('v', "t0")
                        em.emit('v', lambda v, xv=xv, lv=lv: v.tensor_scalar(
                            lv, xv, stat[:, 1:2], stat[:, 4:5], Op.add, Op.mult), [("v0", 1)])
                        em.wait('t', "v0")
                        for cc in range(16):
                            c, par = cc // 2, cc % 2
                            pb = cc % 2
                            a0b = em.count("a0")
                            em.wait('t', "a0", a0b if (u == 0 and cc == 0) else max(0, a0b - 1))
                            em.emit('t', lambda t, lv=lv, c=c, par=par, pb=pb: t.transpose(
                                pst[:, pb * 1024:pb * 1024 + 128],
                                lv[:, 256 * c + par: 256 * (c + 1): 2], cid_bf[:]), [("t0", 1)])
                            em.wait('a', "t0")
                            em.emit('a', lambda a, cc=cc, u=u, pb=pb: a.copy(
                                ln3xT[:, cc * 512 + u * 128: cc * 512 + (u + 1) * 128],
                                pst[:, pb * 1024:pb * 1024 + 128]), [("a0", 1)])
                run_phase(nc, ctrs, ph_ln3)

            with ExitStack() as fb:
                fe = fb.enter_context
                wgbuf = fe(nc.sbuf_tensor(f"wgbuf_{tc}", [128, 2 * 2048], BF16))
                wubuf = fe(nc.sbuf_tensor(f"wubuf_{tc}", [128, 2 * 2048], BF16))
                gsc = fe(nc.sbuf_tensor(f"gsc_{tc}", [128, 4096], F32))
                ps_g = fe(nc.psum_tensor(f"ps_g_{tc}", [128, 1024], F32))
                ps_u = fe(nc.psum_tensor(f"ps_u_{tc}", [128, 1024], F32))

                def ph_gateup(em, tc=tc):
                    fence(em)
                    v0_gelu = {}
                    t1_0 = em.count("t1")
                    for nt in range(NNT):
                        rb = nt % 2
                        for pf in ([0, 1] if nt == 0 else [nt + 1]):
                            if pf < NNT:
                                fsem = "fwA" if pf % 2 == 0 else "fwB"
                                if pf >= 2:
                                    em.wait('y', "t1", t1_0 + pf - 1)
                                em.emit('y', lambda y, pf=pf: y.dma_start(
                                    wgbuf[:, (pf % 2) * 2048:(pf % 2 + 1) * 2048],
                                    sc_wgT[pf * 128:(pf + 1) * 128, :]), [(fsem, 16)])
                                em.emit('y', lambda y, pf=pf: y.dma_start(
                                    wubuf[:, (pf % 2) * 2048:(pf % 2 + 1) * 2048],
                                    sc_wuT[pf * 128:(pf + 1) * 128, :]), [(fsem, 16)])
                        em.wait('t', "fwA" if rb == 0 else "fwB")
                        if nt >= 2:
                            em.wait('t', "v0", v0_gelu[nt - 2])
                        for cc in range(16):
                            em.emit('t', lambda t, cc=cc, rb=rb: t.matmul(
                                ps_g[:, rb * 512:rb * 512 + 512],
                                wgbuf[:, rb * 2048 + cc * 128: rb * 2048 + (cc + 1) * 128],
                                ln3xT[:, cc * 512:(cc + 1) * 512],
                                start=(cc == 0), stop=(cc == 15)), [])
                        for cc in range(16):
                            em.emit('t', lambda t, cc=cc, rb=rb: t.matmul(
                                ps_u[:, rb * 512:rb * 512 + 512],
                                wubuf[:, rb * 2048 + cc * 128: rb * 2048 + (cc + 1) * 128],
                                ln3xT[:, cc * 512:(cc + 1) * 512],
                                start=(cc == 0), stop=(cc == 15)),
                                [("t1", 1)] if cc == 15 else [])
                        gb_ = (nt % 2) * 2048
                        s0 = gsc[:, gb_ + 0:gb_ + 512]
                        s1 = gsc[:, gb_ + 512:gb_ + 1024]
                        s2 = gsc[:, gb_ + 1024:gb_ + 1536]
                        s3 = gsc[:, gb_ + 1536:gb_ + 2048]
                        em.wait('a', "t1")
                        em.emit('a', lambda a, rb=rb: a.activation(s0, ps_g[:, rb * 512:rb * 512 + 512],
                                                                   AF.Square), [("a0", 1)])
                        em.wait('v', "a0")
                        em.emit('v', lambda v: v.tensor_scalar(s1, s0, GC2, GC1, Op.mult, Op.add),
                                [("v0", 1)])
                        em.wait('v', "v0")
                        em.emit('v', lambda v, rb=rb: v.tensor_tensor(
                            s2, ps_g[:, rb * 512:rb * 512 + 512], s1, Op.mult), [("v0", 1)])
                        em.wait('a', "v0")
                        em.emit('a', lambda a: a.activation(s3, s2, AF.Tanh), [("a0", 1)])
                        em.wait('v', "a0")
                        em.emit('v', lambda v: v.tensor_scalar(s1, s3, 0.5, 0.5, Op.mult, Op.add),
                                [("v0", 1)])
                        em.wait('v', "v0")
                        em.emit('v', lambda v, rb=rb: v.tensor_tensor(
                            s2, ps_g[:, rb * 512:rb * 512 + 512], s1, Op.mult), [("v0", 1)])
                        em.wait('v', "v0")
                        em.emit('v', lambda v, rb=rb, nt=nt: v.tensor_tensor(
                            big[:, nt * 512:(nt + 1) * 512], ps_u[:, rb * 512:rb * 512 + 512],
                            s2, Op.mult), [("v0", 1)])
                        v0_gelu[nt] = em.count("v0")
                run_phase(nc, ctrs, ph_gateup)

            with ExitStack() as fc:
                fe = fc.enter_context
                wdring = fe(nc.sbuf_tensor(f"wdring_{tc}", [128, 3 * 2048], BF16))
                dstage = fe(nc.sbuf_tensor(f"dstage_{tc}", [128, 2 * 512], F32))
                xo2 = fe(nc.sbuf_tensor(f"xo2_{tc}", [128, 2 * 2048], F32))
                x2res = fe(nc.sbuf_tensor(f"x2res_{tc}", [128, 2048], F32))
                outt = fe(nc.sbuf_tensor(f"outt_{tc}", [128, 2048], F32))
                ps_d = fe(nc.psum_tensor(f"ps_d_{tc}", [128, 1024], F32))
                ps_t2 = fe(nc.psum_tensor(f"ps_t2_{tc}", [128, 1024], F32))

                def ph_down(em, tc=tc):
                    tmpn2 = sq_scr
                    ln3f = ln3xT[:].bitcast(F32)
                    def xo_ap(u, lo, hi):
                        if u < 2:
                            return ln3f[:, u * 2048 + lo: u * 2048 + hi]
                        return xo2[:, (u - 2) * 2048 + lo: (u - 2) * 2048 + hi]
                    a0_evac = {}
                    t1_0 = em.count("t1")
                    fence(em)
                    for j in range(H):
                        jb = j % 2
                        for bt in range(8):
                            gi = j * 8 + bt
                            for pf in ([0, 1, 2] if gi == 0 else [gi + 2]):
                                if pf < H * 8:
                                    pj, pbt = pf // 8, pf % 8
                                    ps_ = pj * 4 + pbt // 2
                                    dsem = ("dwA", "dwB", "dwC")[pf % 3]
                                    if pf >= 3:
                                        em.wait('y', "t1", t1_0 + pf - 2)
                                    em.emit('y', lambda y, ps_=ps_, pbt=pbt, pf=pf: y.dma_start(
                                        wdring[:, (pf % 3) * 2048:(pf % 3 + 1) * 2048],
                                        sc_wdT[ps_ * 128:(ps_ + 1) * 128,
                                               (pbt % 2) * 2048:(pbt % 2) * 2048 + 2048]),
                                        [(dsem, 16)])
                            em.wait('t', ("dwA", "dwB", "dwC")[gi % 3])
                            if j >= 2 and bt == 0:
                                em.wait('t', "a0", a0_evac[j - 2])
                            for u in range(16):
                                i = bt * 16 + u
                                em.emit('t', lambda t, u=u, i=i, gi=gi, jb=jb: t.matmul(
                                    ps_d[:, jb * 512:jb * 512 + 512],
                                    wdring[:, (gi % 3) * 2048 + u * 128:(gi % 3) * 2048 + (u + 1) * 128],
                                    big[:, i * 512:(i + 1) * 512],
                                    start=(i == 0), stop=(i == NNT - 1)),
                                    [("t1", 1)] if u == 15 else [])
                        em.wait('a', "t1")
                        em.emit('a', lambda a, jb=jb: a.copy(
                            dstage[:, jb * 512:(jb + 1) * 512], ps_d[:, jb * 512:jb * 512 + 512]),
                            [("a0", 1)])
                        a0_evac[j] = em.count("a0")
                        for u in range(4):
                            pb = u % 2
                            v0b = em.count("v0")
                            em.wait('t', "v0", v0b if (j == 0 and u == 0) else max(0, v0b - 1))
                            em.wait('t', "a0")
                            em.emit('t', lambda t, u=u, jb=jb, pb=pb: t.transpose(
                                ps_t2[:, pb * 512:pb * 512 + 128],
                                dstage[:, jb * 512 + u * 128: jb * 512 + (u + 1) * 128], cid_f[:]),
                                [("t0", 1)])
                            em.wait('v', "t0")
                            em.emit('v', lambda v, u=u, j=j, pb=pb: v.tensor_copy(
                                xo_ap(u, j * 128, (j + 1) * 128),
                                ps_t2[:, pb * 512:pb * 512 + 128]), [("v0", 1)])
                    for u in range(4):
                        row0 = tc * 512 + u * 128
                        xo = xo_ap(u, 0, 2048)
                        em.wait('y', "v0")
                        em.emit('y', lambda y, row0=row0: y.dma_start(
                            x2res[:], sc_x2[row0:row0 + 128, :]), [("lx", 16)])
                        em.wait('v', "v0")
                        em.wait('a', "v0")
                        ln_stats(em, xo, rstd_col=4, nmean_col=1, n_elems=D, sq_slice=sq_scr[:],
                                 sqcol=5 + (u % 2) * 2)
                        em.wait('v', "a0")
                        em.emit('v', lambda v, xo=xo: v.tensor_scalar(
                            tmpn2[:, 0:2048], xo, stat[:, 1:2], stat[:, 4:5], Op.add, Op.mult), [("v0", 1)])
                        em.wait('v', "lx")
                        em.wait('v', "st")
                        em.wait('v', "v0")
                        em.emit('v', lambda v: v.tensor_add(outt[:], tmpn2[:, 0:2048], x2res[:]), [("v0", 1)])
                        em.wait('y', "v0")
                        em.emit('y', lambda y, row0=row0: y.dma_start(
                            out_h[row0:row0 + 128, :], outt[:]), [("st", 16)])
                run_phase(nc, ctrs, ph_down)

        def ph_fin(em):
            em.wait('y', "st")
            em.wait('t', "st")
        run_phase(nc, ctrs, ph_fin)
        return nc, dbg_h


def _finish(nc, ctrs, out_h, big):
    def ph(em):
        em.emit('y', lambda y: y.dma_start(out_h[0:128, :], big[:, 0:4096].bitcast(F32)),
                [("st", 16)])
        em.wait('y', "st")
    run_phase(nc, ctrs, ph)


def _consts():
    cid_bf = np.eye(128, dtype=BF16NP)
    cid_f = np.eye(128, dtype=np.float32)
    p = np.arange(128, dtype=np.float32)[:, None]
    j = np.arange(512, dtype=np.float32)[None, :]
    r0 = p - j
    planes = [r0]
    for c in (0.0, 128.0, 256.0, 384.0):
        m = np.where(r0 + c > 0, np.float32(-1e33), np.float32(0.0))
        planes.append(r0 + m)
    cr0 = np.concatenate(planes, axis=1).astype(np.float32)
    def mplane(c):
        return np.where(r0 + c > 0, np.float32(-1e33), np.float32(0.0))
    pl = np.concatenate([r0, r0 + 128.0, r0 + mplane(0.0), r0 + 128.0 + mplane(128.0),
                         r0 + mplane(256.0), r0 + 128.0 + mplane(384.0)], axis=1).astype(np.float32)
    hi = pl.astype(BF16NP)
    lo = (pl - hi.astype(np.float32)).astype(BF16NP)
    cr3 = np.concatenate([hi, lo], axis=1)
    cdg = np.zeros((128, H * 128), dtype=BF16NP)
    for h in range(H):
        cdg[:, h * 128:(h + 1) * 128] = (np.eye(128, dtype=np.float32)
                                         * np.float32(ALIBI_SLOPES[h])).astype(BF16NP)
    ceb = np.zeros((128, H * 16), dtype=np.float32)
    for h in range(H):
        for idx in range(16):
            coff = np.float32(128.0) * np.float32(idx - 12)
            ceb[:, h * 16 + idx] = np.float32(ALIBI_SLOPES[h]) * coff
    cones = np.ones((128, 1), dtype=BF16NP)
    conesrow = np.ones((1, 128), dtype=BF16NP)
    cmisc = np.zeros((128, NMISC), dtype=np.float32)
    cmisc[:, MC_EPS] = EPS
    cmisc[:, MC_NHALF] = -0.5
    cmisc[:, MC_QS] = QS
    return dict(cid_bf=cid_bf, cid_f=cid_f, cr0=cr0, cr3=cr3, cdg=cdg, ceb=ceb, cones=cones,
                conesrow=conesrow, cmisc=cmisc)


def _pack_u8(a_int32):
    a = np.ascontiguousarray(a_int32)
    return np.ascontiguousarray(a.view(np.uint8).reshape(a.shape[0], a.shape[1], 4)[:, :, 0])


def make_in_maps(L0_attn_x, L0_attn_Wq, L0_attn_Wk, L0_attn_Wv, L0_attn_Wo,
                 L0_ffn_gate_W_packed, L0_ffn_gate_scales, L0_ffn_up_W_packed,
                 L0_ffn_up_scales, L0_ffn_down_W_packed, L0_ffn_down_scales):
    consts = _consts()
    shared = dict(
        wq=np.ascontiguousarray(L0_attn_Wq, dtype=np.float32),
        wk=np.ascontiguousarray(L0_attn_Wk, dtype=np.float32),
        wv=np.ascontiguousarray(L0_attn_Wv, dtype=np.float32),
        wo=np.ascontiguousarray(L0_attn_Wo, dtype=np.float32),
        gpk=_pack_u8(L0_ffn_gate_W_packed),
        gsc=np.ascontiguousarray(L0_ffn_gate_scales, dtype=np.float32),
        upk=_pack_u8(L0_ffn_up_W_packed),
        usc=np.ascontiguousarray(L0_ffn_up_scales, dtype=np.float32),
        dpk=_pack_u8(L0_ffn_down_W_packed),
        dsc=np.ascontiguousarray(L0_ffn_down_scales, dtype=np.float32),
        **consts,
    )
    x = np.ascontiguousarray(L0_attn_x, dtype=np.float32).reshape(16, T, D)
    in_maps = []
    for c in range(8):
        m = dict(shared)
        m["x"] = np.ascontiguousarray(x[2 * c:2 * c + 2].reshape(TOK, D))
        in_maps.append(m)
    return in_maps


def kernel(**inputs):
    in_maps = make_in_maps(**inputs)
    nc, _ = build()
    res = run_bass_kernel_spmd(nc, in_maps, core_ids=list(range(8)))
    out = np.concatenate([r["out"].reshape(2, T, D) for r in res.results], axis=0)
    return out


# version 16
# speedup vs baseline: 1.1938x; 1.0518x over previous
import numpy as np
import ml_dtypes
from contextlib import ExitStack

import concourse.bass as bass
import concourse.mybir as mybir
from concourse.bass_utils import run_bass_kernel_spmd
from concourse.alu_op_type import AluOpType as Op

F32 = mybir.dt.float32
BF16 = mybir.dt.bfloat16
I16 = mybir.dt.int16
U8 = mybir.dt.uint8
AF = mybir.ActivationFunctionType
AX = mybir.AxisListType.X
BF16NP = ml_dtypes.bfloat16

D = 2048
TOK = 4096
T = 2048
NT = TOK // 128
H = 16
HD = 128
MLP = 16384
NNT = MLP // 128
KC = D // 128
EPS = 1e-5
QS = float(1.0 / np.sqrt(np.float32(128.0)))
GC1 = 0.7978845608028654
GC2 = 0.7978845608028654 * 0.044715

NMISC = 8
MC_EPS = 0
MC_NHALF = 1
MC_QS = 2
MC_ZERO = 3
MC_TWO = 4
MC_NQB = 5

ALIBI_SLOPES = [float(np.float32(0.5) ** (np.float32(h + 1) * np.float32(0.5))) for h in range(H)]


class Ctr:
    __slots__ = ("h", "c")
    def __init__(self, h):
        self.h = h
        self.c = 0


class Em:
    def __init__(self, which, eng, ctrs):
        self.w = which
        self.e = eng
        self.c = ctrs

    def emit(self, tag, f, incs=()):
        ins = None
        if tag == self.w:
            ins = f(self.e)
            for (s, n) in incs:
                ins = ins.then_inc(self.c[s].h, n)
        for (s, n) in incs:
            self.c[s].c += n
        return ins

    def wait(self, tag, s, val=None):
        if tag == self.w:
            self.e.wait_ge(self.c[s].h, self.c[s].c if val is None else val)

    def count(self, s):
        return self.c[s].c


def run_phase(nc, ctrs, body):
    base = {k: c.c for k, c in ctrs.items()}
    with nc.Block() as block:
        def mk(tag):
            def sec(eng):
                for k, v0 in base.items():
                    ctrs[k].c = v0
                body(Em(tag, eng, ctrs))
            sec.__name__ = f"sec_{tag}"
            return sec
        block.tensor(mk('t'))
        block.vector(mk('v'))
        block.scalar(mk('a'))
        block.sync(mk('y'))
        block.gpsimd(mk('g'))


def build(stop_after=None, debug=()):
    nc = bass.Bass("TRN2", target_bir_lowering=False)
    dbg = set(debug)

    x_h = nc.dram_tensor("x", [TOK, D], F32, kind="ExternalInput").ap()
    wq_h = nc.dram_tensor("wq", [D, D], F32, kind="ExternalInput").ap()
    wk_h = nc.dram_tensor("wk", [D, D], F32, kind="ExternalInput").ap()
    wv_h = nc.dram_tensor("wv", [D, D], F32, kind="ExternalInput").ap()
    wo_h = nc.dram_tensor("wo", [D, D], F32, kind="ExternalInput").ap()
    gpk_h = nc.dram_tensor("gpk", [MLP, D // 2], U8, kind="ExternalInput").ap()
    gsc_h = nc.dram_tensor("gsc", [MLP, D // 32], F32, kind="ExternalInput").ap()
    upk_h = nc.dram_tensor("upk", [MLP, D // 2], U8, kind="ExternalInput").ap()
    usc_h = nc.dram_tensor("usc", [MLP, D // 32], F32, kind="ExternalInput").ap()
    dpk_h = nc.dram_tensor("dpk", [D, MLP // 2], U8, kind="ExternalInput").ap()
    dsc_h = nc.dram_tensor("dsc", [D, MLP // 32], F32, kind="ExternalInput").ap()
    cid_bf_h = nc.dram_tensor("cid_bf", [128, 128], BF16, kind="ExternalInput").ap()
    cid_f_h = nc.dram_tensor("cid_f", [128, 128], F32, kind="ExternalInput").ap()
    cr3_h = nc.dram_tensor("cr3", [128, 12 * 512], BF16, kind="ExternalInput").ap()
    cdg_h = nc.dram_tensor("cdg", [128, H * 128], BF16, kind="ExternalInput").ap()
    ceb_h = nc.dram_tensor("ceb", [128, H * 16], F32, kind="ExternalInput").ap()
    cones_h = nc.dram_tensor("cones", [128, 1], BF16, kind="ExternalInput").ap()
    conesrow_h = nc.dram_tensor("conesrow", [1, 128], BF16, kind="ExternalInput").ap()
    cmisc_h = nc.dram_tensor("cmisc", [128, NMISC], F32, kind="ExternalInput").ap()

    out_h = nc.dram_tensor("out", [TOK, D], F32, kind="ExternalOutput").ap()

    sc_avT = nc.dram_tensor("sc_avT", [D, TOK], BF16, kind="Internal").ap()
    sc_x2 = nc.dram_tensor("sc_x2", [TOK, D], F32, kind="Internal").ap()
    sc_wgT = nc.dram_tensor("sc_wgT", [MLP, D], BF16, kind="Internal").ap()
    sc_wuT = nc.dram_tensor("sc_wuT", [MLP, D], BF16, kind="Internal").ap()
    sc_wdT = nc.dram_tensor("sc_wdT", [64 * 128, 4096], BF16, kind="Internal").ap()

    dbg_h = {}
    def dbg_tensor(name, shape, dtype):
        if name in dbg:
            dbg_h[name] = nc.dram_tensor("dbg_" + name, shape, dtype, kind="ExternalOutput").ap()
            return dbg_h[name]
        return None

    with ExitStack() as ctx:
        e = ctx.enter_context
        big = e(nc.sbuf_tensor("big", [128, 65536], BF16))
        cid_bf = e(nc.sbuf_tensor("sb_cid_bf", [128, 128], BF16))
        cid_f = e(nc.sbuf_tensor("sb_cid_f", [128, 128], F32))
        cones = e(nc.sbuf_tensor("sb_cones", [128, 1], BF16))
        conesrow = e(nc.sbuf_tensor("sb_conesrow", [1, 128], BF16))
        cmisc = e(nc.sbuf_tensor("sb_cmisc", [128, NMISC], F32))
        stat = e(nc.sbuf_tensor("stat", [128, 16], F32))
        sq_scr = e(nc.sbuf_tensor("sq_scr", [128, 2048], F32))
        wrow_ctx = ExitStack()
        wrow = wrow_ctx.enter_context(nc.sbuf_tensor("wrow", [128, 2 * 2048], BF16))

        sem_names = ["ly", "lyA", "lyB", "lw", "lx", "lg", "st", "stA", "stB",
                     "spA", "spB", "fwA", "fwB", "dwA", "dwB", "dwC", "wtr", "vtr",
                     "t0", "t1", "v0", "a0"]
        ctrs = {n: Ctr(e(nc.semaphore(n))) for n in sem_names}

        meta = {"ctrs": ctrs}

        def ph_consts(em):
            em.emit('y', lambda y: y.dma_start(cid_bf[:], cid_bf_h), [("ly", 16)])
            em.emit('y', lambda y: y.dma_start(cid_f[:], cid_f_h), [("ly", 16)])
            em.emit('y', lambda y: y.dma_start(cones[:], cones_h), [("ly", 16)])
            em.emit('y', lambda y: y.dma_start(conesrow[:], conesrow_h), [("ly", 16)])
            em.emit('y', lambda y: y.dma_start(cmisc[:], cmisc_h), [("ly", 16)])
            em.wait('y', "ly")
        run_phase(nc, ctrs, ph_consts)

        def fence(em):
            for tg in ('t', 'v', 'a', 'y', 'g'):
                for sm in ("st", "stA", "stB", "spA", "spB"):
                    em.wait(tg, sm)

        lctx = ExitStack()
        pst = lctx.enter_context(nc.psum_tensor("pst_l", [128, 2048], BF16))
        xin = lctx.enter_context(nc.sbuf_tensor("xin", [128, 2 * 2048], F32))
        ln1o = lctx.enter_context(nc.sbuf_tensor("ln1o", [128, 2 * 2048], BF16))

        def ln_stats(em, src_ap, rstd_col, nmean_col, n_elems, sq_slice, sqcol=5):
            em.emit('v', lambda v: v.reduce_sum(stat[:, 0:1], src_ap, AX), [("v0", 1)])
            em.emit('a', lambda a: a.activation(sq_slice, src_ap, AF.Square,
                                                accum_out=stat[:, sqcol:sqcol + 1]), [("a0", 1)])
            em.wait('v', "v0")
            em.emit('v', lambda v: v.tensor_scalar_mul(stat[:, nmean_col:nmean_col + 1],
                                                       stat[:, 0:1], -1.0 / n_elems), [("v0", 1)])
            em.wait('v', "v0")
            em.emit('v', lambda v: v.tensor_mul(stat[:, 2:3], stat[:, nmean_col:nmean_col + 1],
                                                stat[:, nmean_col:nmean_col + 1]), [("v0", 1)])
            em.wait('v', "a0")
            em.wait('v', "v0")
            em.emit('v', lambda v: v.scalar_tensor_tensor(stat[:, 3:4], stat[:, sqcol:sqcol + 1],
                                                          1.0 / n_elems, stat[:, 2:3],
                                                          Op.mult, Op.subtract), [("v0", 1)])
            em.wait('a', "v0")
            em.emit('a', lambda a: a.activation(stat[:, 6:7], stat[:, 3:4], AF.Ln,
                                                bias=cmisc[:, MC_EPS:MC_EPS + 1]), [("a0", 1)])
            em.wait('a', "a0")
            em.emit('a', lambda a: a.activation(stat[:, rstd_col:rstd_col + 1], stat[:, 6:7], AF.Exp,
                                                scale=cmisc[:, MC_NHALF:MC_NHALF + 1]), [("a0", 1)])

        def ph_ln1(em):
            for i in range(NT):
                b = i % 2
                xv = xin[:, b * 2048:(b + 1) * 2048]
                lv = ln1o[:, b * 2048:(b + 1) * 2048]
                if i >= 2:
                    em.wait('y', "t0", (i - 2 + 1) * KC)
                lsem = "lyA" if i % 2 == 0 else "lyB"
                em.emit('y', lambda y, i=i, xv=xv: y.dma_start(xv, x_h[i * 128:(i + 1) * 128, :]),
                        [(lsem, 16)])
                em.wait('v', lsem)
                em.wait('a', lsem)
                ln_stats(em, xv, rstd_col=4, nmean_col=1, n_elems=D, sq_slice=sq_scr[:])
                em.wait('v', "a0")
                em.emit('v', lambda v, xv=xv, lv=lv: v.tensor_scalar(
                    lv, xv, stat[:, 1:2], stat[:, 4:5], Op.add, Op.mult), [("v0", 1)])
                em.wait('t', "v0")
                for c in range(KC):
                    pb = c % 2
                    a0_before = em.count("a0")
                    if a0_before > 0:
                        em.wait('t', "a0", max(0, a0_before - 1))
                    em.emit('t', lambda t, lv=lv, c=c, pb=pb: t.transpose(
                        pst[:, pb * 1024:pb * 1024 + 128], lv[:, c * 128:(c + 1) * 128], cid_bf[:]),
                        [("t0", 1)])
                    em.wait('a', "t0")
                    em.emit('a', lambda a, c=c, i=i, pb=pb: a.copy(
                        big[:, c * 4096 + i * 128: c * 4096 + (i + 1) * 128],
                        pst[:, pb * 1024:pb * 1024 + 128]), [("a0", 1)])
        run_phase(nc, ctrs, ph_ln1)
        lctx.close()

        if "ln1xt" in dbg:
            tap = dbg_tensor("ln1xt", [128, 65536], BF16)
            def ph_dbg1(em):
                em.emit('y', lambda y: y.dma_start(tap, big[:]), [("st", 16)])
                em.wait('y', "st")
            run_phase(nc, ctrs, ph_dbg1)

        if stop_after == "ln1":
            _finish(nc, ctrs, out_h, big)
            return nc, dbg_h

        with ExitStack() as actx:
            ae = actx.enter_context
            ceb = ae(nc.sbuf_tensor("sb_ceb", [128, H * 16], F32))
            wT = ae(nc.sbuf_tensor("wT", [128, 3 * 2048], BF16))
            qT = ae(nc.sbuf_tensor("qT", [128, TOK], BF16))
            kT = ae(nc.sbuf_tensor("kT", [128, TOK], BF16))
            vstage = ae(nc.sbuf_tensor("vstage", [128, 512], BF16))
            vtok = ae(nc.sbuf_tensor("vtok", [128, TOK], BF16))
            ptile = ae(nc.sbuf_tensor("ptile", [128, 4 * 512], BF16))
            recbf = ae(nc.sbuf_tensor("recbf", [1, 2 * 512], BF16))
            plane = ae(nc.sbuf_tensor("plane", [128, 2 * 512], BF16))
            avsb = ae(nc.sbuf_tensor("avsb", [128, 2 * 512], BF16))
            cr3 = ae(nc.sbuf_tensor("sb_cr3", [128, 12 * 512], BF16))
            cdg = ae(nc.sbuf_tensor("sb_cdg", [128, 2 * 128], BF16))
            ps_a = ae(nc.psum_tensor("ps_a", [128, 1024], F32))
            ps_b = ae(nc.psum_tensor("ps_b", [128, 1024], F32))
            ps_av = ae(nc.psum_tensor("ps_av", [128, 1024], F32))
            ps_sum = ae(nc.psum_tensor("ps_sum", [128, 1024], F32))

            def ph_attn_consts(em):
                fence(em)
                em.emit('y', lambda y: y.dma_start(cr3[:], cr3_h), [("ly", 16)])
                em.emit('y', lambda y: y.dma_start(ceb[:], ceb_h), [("ly", 16)])
                em.wait('y', "ly")
            run_phase(nc, ctrs, ph_attn_consts)

            w_hs = {"q": wq_h, "k": wk_h, "v": wv_h}

            def ph_head(em, h, trk):
                sbank = [ps_a, ps_b]
                hp = h % 2
                em.wait('y', "t0", trk.get(("cdgdone", hp), 0))
                em.emit('y', lambda y, h=h, hp=hp: y.dma_start(
                    cdg[:, hp * 128:(hp + 1) * 128], cdg_h[:, h * 128:(h + 1) * 128]), [("lw", 16)])
                em.wait('t', "a0", trk.get("last_exp", 0))
                for pi in range(3):
                    pn = ("q", "k", "v")[pi]
                    gpar = (h * 3 + pi) % 2
                    em.wait('g', "a0", trk.get(("wrow_done", gpar), 0))
                    em.emit('g', lambda g, pn=pn, h=h, gpar=gpar: g.dma_start(
                        wrow[:, gpar * 2048:(gpar + 1) * 2048],
                        w_hs[pn][h * 128:(h + 1) * 128, :]), [("lg", 16)])
                    em.wait('t', "lg")
                    em.wait('t', "t0", trk.get(("qkv_t0", pi), 0))
                    for grp in range(2):
                        bank = ps_a if grp == 0 else ps_b
                        bb16 = bank[:, 0:512].bitcast(BF16)
                        if grp == 1:
                            pass
                        ga_n = trk.get("wgrp_n", 0)
                        if ga_n >= 2:
                            em.wait('t', "a0", trk[("wgrp", ga_n - 2)])
                        for i in range(8):
                            c = grp * 8 + i
                            em.emit('t', lambda t, c=c, i=i, bb16=bb16, gpar=gpar: t.transpose(
                                bb16[:, i * 128:(i + 1) * 128],
                                wrow[:, gpar * 2048 + c * 128: gpar * 2048 + (c + 1) * 128],
                                cid_bf[:]), [("t0", 1)] if i == 7 else [])
                        em.wait('a', "t0")
                        em.emit('a', lambda a, pi=pi, grp=grp, bb16=bb16: a.copy(
                            wT[:, pi * 2048 + grp * 1024: pi * 2048 + (grp + 1) * 1024],
                            bb16[:]), [("a0", 1)])
                        trk[("wgrp", ga_n)] = em.count("a0")
                        trk["wgrp_n"] = ga_n + 1
                    trk[("wrow_done", gpar)] = em.count("a0")
                em.wait('t', "a0")
                for pi, pn in enumerate(("q", "k", "v")):
                    for tc in range(8):
                        bb = tc % 2
                        a0b = em.count("a0")
                        em.wait('t', "a0", max(0, a0b - 1))
                        for c in range(KC):
                            em.emit('t', lambda t, pi=pi, tc=tc, c=c, bb=bb: t.matmul(
                                ps_a[:, bb * 512:bb * 512 + 512],
                                wT[:, pi * 2048 + c * 128: pi * 2048 + (c + 1) * 128],
                                big[:, c * 4096 + tc * 512:(c * 4096 + (tc + 1) * 512)],
                                start=(c == 0), stop=(c == KC - 1)),
                                [("t0", 1)] if c == KC - 1 else [])
                        em.wait('a', "t0")
                        if pn == "q":
                            em.emit('a', lambda a, tc=tc, bb=bb: a.activation(
                                qT[:, tc * 512:(tc + 1) * 512], ps_a[:, bb * 512:bb * 512 + 512],
                                AF.Identity, scale=cmisc[:, MC_QS:MC_QS + 1]), [("a0", 1)])
                        elif pn == "k":
                            em.emit('a', lambda a, tc=tc, bb=bb: a.copy(
                                kT[:, tc * 512:(tc + 1) * 512], ps_a[:, bb * 512:bb * 512 + 512]),
                                [("a0", 1)])
                        else:
                            em.wait('a', "vtr")
                            em.emit('a', lambda a, tc=tc, bb=bb: a.copy(
                                vstage[:], ps_a[:, bb * 512:bb * 512 + 512]), [("a0", 1)])
                            em.wait('y', "a0")
                            for u in range(4):
                                em.emit('y', lambda y, tc=tc, u=u: y.dma_start(
                                    vtok[:, tc * 512 + u * 128: tc * 512 + (u + 1) * 128],
                                    vstage[:, u * 128:(u + 1) * 128], transpose=True),
                                    [("vtr", 16)])
                    if pn == "v":
                        pass
                    trk[("qkv_t0", pi)] = em.count("t0")
                em.wait('t', "a0")
                em.wait('t', "vtr")
                em.wait('t', "lw")
                batches = []
                for g_ in range(4):
                    nb = 2 * (g_ + 1)
                    for b_ in range(nb):
                        for s_ in range(2):
                            batches.append((s_, g_, 2 * b_, b_ == nb - 1))
                exp_pos = {}
                av_pos = {}
                for bi, (s_, g_, kt0, lastb) in enumerate(batches):
                    pb2 = bi % 2
                    sb_ = sbank[pb2]
                    didx = kt0 - 4 * g_ + 12
                    if bi >= 2:
                        em.wait('t', "a0", exp_pos[bi - 2])
                    if kt0 == 0:
                        em.wait('t', "v0", trk.get(("plane_s", s_), 0))
                        em.wait('t', "v0", trk.get(("norm_s", s_), 0))
                    for half in range(2):
                        kt = kt0 + half
                        if kt < 4 * g_:
                            pcol = half * 512
                        else:
                            pcol = (2 + (kt - 4 * g_)) * 512
                        em.emit('t', lambda t, s_=s_, g_=g_, kt=kt, sb_=sb_, half=half: t.matmul(
                            sb_[:, half * 512:(half + 1) * 512],
                            kT[:, s_ * 2048 + kt * 128: s_ * 2048 + (kt + 1) * 128],
                            qT[:, s_ * 2048 + g_ * 512: s_ * 2048 + (g_ + 1) * 512],
                            start=True, stop=False), [])
                        em.emit('t', lambda t, sb_=sb_, half=half, h=h, pcol=pcol: t.matmul(
                            sb_[:, half * 512:(half + 1) * 512],
                            cdg[:, hp * 128:(hp + 1) * 128],
                            cr3[:, pcol:pcol + 512],
                            start=False, stop=False), [])
                        em.emit('t', lambda t, sb_=sb_, half=half, h=h, pcol=pcol: t.matmul(
                            sb_[:, half * 512:(half + 1) * 512],
                            cdg[:, hp * 128:(hp + 1) * 128],
                            cr3[:, 3072 + pcol:3072 + pcol + 512],
                            start=False, stop=True), [("t0", 1)] if half == 1 else [])
                    mmp = em.count("t0")
                    em.wait('a', "t0", mmp)
                    if bi >= 2:
                        em.wait('a', "t0", av_pos[bi - 2])
                    em.emit('a', lambda a, pb2=pb2, h=h, didx=didx, sb_=sb_: a.activation(
                        ptile[:, pb2 * 1024:(pb2 + 1) * 1024], sb_[:],
                        AF.Exp, bias=ceb[:, h * 16 + didx: h * 16 + didx + 1]), [("a0", 1)])
                    exp_pos[bi] = em.count("a0")
                    em.wait('t', "a0", exp_pos[bi])
                    nkt = 4 * (g_ + 1)
                    for half in range(2):
                        kt = kt0 + half
                        em.emit('t', lambda t, kt=kt, pb2=pb2, half=half, s_=s_, nkt=nkt: t.matmul(
                            ps_sum[0:1, s_ * 512:s_ * 512 + 512], cones[:],
                            ptile[:, pb2 * 1024 + half * 512: pb2 * 1024 + (half + 1) * 512],
                            start=(kt == 0), stop=(kt == nkt - 1)), [])
                        em.emit('t', lambda t, s_=s_, kt=kt, pb2=pb2, half=half, nkt=nkt: t.matmul(
                            ps_av[:, s_ * 512:s_ * 512 + 512],
                            vtok[:, (s_ * 16 + kt) * 128:(s_ * 16 + kt + 1) * 128],
                            ptile[:, pb2 * 1024 + half * 512: pb2 * 1024 + (half + 1) * 512],
                            start=(kt == 0), stop=(kt == nkt - 1)),
                            [("t0", 1)] if half == 1 else [])
                    av_pos[bi] = em.count("t0")
                    if lastb:
                        pp = s_
                        em.wait('v', "t0", av_pos[bi])
                        def _recip(v, pp=pp):
                            with nc.allow_low_precision(reason="softmax 1/S bf16; LN scale-inv"):
                                return v.reciprocal(recbf[:, pp * 512:(pp + 1) * 512],
                                                    ps_sum[0:1, pp * 512:pp * 512 + 512])
                        em.emit('v', _recip, [("v0", 1)])
                        rp = em.count("v0")
                        em.wait('t', "v0", rp)
                        em.emit('t', lambda t, pp=pp: t.matmul(
                            ps_sum[:, pp * 512:pp * 512 + 512], conesrow[:],
                            recbf[:, pp * 512:(pp + 1) * 512], start=True, stop=True), [("t0", 1)])
                        plp = em.count("t0")
                        em.wait('v', "t0", plp)
                        em.emit('v', lambda v, pp=pp: v.tensor_copy(
                            plane[:, pp * 512:(pp + 1) * 512], ps_sum[:, pp * 512:pp * 512 + 512]),
                            [("v0", 1)])
                        trk[("plane_s", s_)] = em.count("v0")
                        em.wait('v', "v0", trk[("plane_s", s_)])
                        em.wait('v', "stA" if pp == 0 else "stB")
                        em.emit('v', lambda v, pp=pp: v.tensor_mul(
                            avsb[:, pp * 512:(pp + 1) * 512],
                            ps_av[:, pp * 512:pp * 512 + 512],
                            plane[:, pp * 512:(pp + 1) * 512]), [("v0", 1)])
                        trk[("norm_s", s_)] = em.count("v0")
                        em.wait('y', "v0", trk[("norm_s", s_)])
                        em.emit('y', lambda y, h=h, s_=s_, g_=g_, pp=pp: y.dma_start(
                            sc_avT[h * 128:(h + 1) * 128,
                                   s_ * 2048 + g_ * 512: s_ * 2048 + (g_ + 1) * 512],
                            avsb[:, pp * 512:(pp + 1) * 512]),
                            [("stA" if pp == 0 else "stB", 16)])
                trk["last_exp"] = exp_pos[len(batches) - 1]
                trk[("cdgdone", hp)] = em.count("t0")

            def ph_attention(em):
                trk = {}
                for h in range(H):
                    ph_head(em, h, trk)
            run_phase(nc, ctrs, ph_attention)

        if "avt" in dbg:
            tap = dbg_tensor("avt", [D, TOK], BF16)
            def ph_dbga(em):
                for r in range(4):
                    em.emit('y', lambda y, r=r: y.dma_start(
                        tap[r * 512:(r + 1) * 512, :], sc_avT[r * 512:(r + 1) * 512, :]),
                        [("st", 16)])
                em.wait('y', "st")
            run_phase(nc, ctrs, ph_dbga)

        if stop_after == "attn":
            _finish(nc, ctrs, out_h, big)
            return nc, dbg_h

        bigf = big[:].bitcast(F32)
        with ExitStack() as wctx:
            we = wctx.enter_context
            projst = we(nc.sbuf_tensor("projst", [128, 2 * 512], F32))
            x2t = we(nc.sbuf_tensor("x2t", [128, 2048], F32))
            xres = we(nc.sbuf_tensor("xres", [128, 2048], F32))
            tmpn = we(nc.sbuf_tensor("tmpn", [128, 2048], F32))
            ps_p = we(nc.psum_tensor("ps_p", [128, 1024], F32))
            ps_tr = we(nc.psum_tensor("ps_tr", [128, 1024], F32))

            def ph_wo_prep(em):
                fence(em)
                banks16 = [ps_p[:, 0:512].bitcast(BF16), ps_tr[:, 0:512].bitcast(BF16)]
                grp_a0 = []
                wrow_done = {}
                for m in range(16):
                    gpar = m % 2
                    em.wait('g', "a0", wrow_done.get(gpar, 0))
                    em.emit('g', lambda g, m=m, gpar=gpar: g.dma_start(
                        wrow[:, gpar * 2048:(gpar + 1) * 2048],
                        wo_h[m * 128:(m + 1) * 128, :]), [("lg", 16)])
                    em.wait('t', "lg")
                    for grp in range(2):
                        bb16 = banks16[grp]
                        if len(grp_a0) >= 2:
                            em.wait('t', "a0", grp_a0[-2])
                        for i in range(8):
                            c = grp * 8 + i
                            em.emit('t', lambda t, c=c, i=i, bb16=bb16, gpar=gpar: t.transpose(
                                bb16[:, i * 128:(i + 1) * 128],
                                wrow[:, gpar * 2048 + c * 128: gpar * 2048 + (c + 1) * 128],
                                cid_bf[:]), [("t0", 1)] if i == 7 else [])
                        em.wait('a', "t0")
                        base = 8192 + grp * 8 * 2048
                        dst = big[:, base: base + 8 * 2048].rearrange(
                            "p (c w) -> p c w", w=2048)[:, :, m * 128:(m + 1) * 128]
                        em.emit('a', lambda a, dst=dst, bb16=bb16: a.copy(
                            dst, bb16[:].rearrange("p (c w) -> p c w", w=128)), [("a0", 1)])
                        grp_a0.append(em.count("a0"))
                    wrow_done[gpar] = em.count("a0")
                em.wait('t', "a0")
                em.wait('v', "a0")
            run_phase(nc, ctrs, ph_wo_prep)

            def ph_wo_tc(em, tc):
                em.wait('y', "stA")
                em.wait('y', "stB")
                em.emit('y', lambda y, tc=tc: y.dma_start(
                    big[:, 0:8192],
                    sc_avT.rearrange("(c p) t -> p c t", p=128)[:, :, tc * 512:(tc + 1) * 512]),
                    [("lw", 16)])
                em.wait('t', "lw")
                for m in range(16):
                    bb = m % 2
                    a0b = em.count("a0")
                    em.wait('t', "a0", max(0, a0b - 1))
                    for dch in range(16):
                        em.emit('t', lambda t, m=m, dch=dch, bb=bb: t.matmul(
                            ps_p[:, bb * 512:bb * 512 + 512],
                            big[:, 8192 + dch * 2048 + m * 128: 8192 + dch * 2048 + (m + 1) * 128],
                            big[:, dch * 512:(dch + 1) * 512],
                            start=(dch == 0), stop=(dch == KC - 1)),
                            [("t0", 1)] if dch == KC - 1 else [])
                    em.wait('a', "t0")
                    em.emit('a', lambda a, m=m, bb=bb: a.copy(
                        projst[:, bb * 512:(bb + 1) * 512], ps_p[:, bb * 512:bb * 512 + 512]),
                        [("a0", 1)])
                    for u in range(4):
                        pb = u % 2
                        v0b = em.count("v0")
                        em.wait('t', "v0", max(0, v0b - 1))
                        em.wait('t', "a0")
                        em.emit('t', lambda t, u=u, pb=pb, bb=bb: t.transpose(
                            ps_tr[:, pb * 512:pb * 512 + 128],
                            projst[:, bb * 512 + u * 128: bb * 512 + (u + 1) * 128], cid_f[:]),
                            [("t0", 1)])
                        em.wait('v', "t0")
                        em.emit('v', lambda v, u=u, m=m, pb=pb: v.tensor_copy(
                            bigf[:, 20480 + u * 2048 + m * 128: 20480 + u * 2048 + (m + 1) * 128],
                            ps_tr[:, pb * 512:pb * 512 + 128]), [("v0", 1)])
                for u in range(4):
                    row0 = tc * 512 + u * 128
                    xo = bigf[:, 20480 + u * 2048: 20480 + (u + 1) * 2048]
                    em.wait('y', "v0")
                    em.emit('y', lambda y, row0=row0: y.dma_start(
                        xres[:], x_h[row0:row0 + 128, :]), [("lx", 16)])
                    em.wait('v', "v0")
                    em.wait('a', "v0")
                    ln_stats(em, xo, rstd_col=4, nmean_col=1, n_elems=D, sq_slice=sq_scr[:])
                    em.wait('v', "a0")
                    em.emit('v', lambda v, xo=xo: v.tensor_scalar(
                        tmpn[:], xo, stat[:, 1:2], stat[:, 4:5], Op.add, Op.mult), [("v0", 1)])
                    em.wait('v', "lx")
                    em.wait('v', "st")
                    em.wait('v', "v0")
                    em.emit('v', lambda v: v.tensor_add(x2t[:], tmpn[:], xres[:]), [("v0", 1)])
                    em.wait('y', "v0")
                    em.emit('y', lambda y, row0=row0: y.dma_start(
                        sc_x2[row0:row0 + 128, :], x2t[:]), [("st", 16)])

            for tc in range(8):
                run_phase(nc, ctrs, lambda em, tc=tc: ph_wo_tc(em, tc))

        if "x2" in dbg:
            tap = dbg_tensor("x2", [TOK, D], F32)
            def ph_dbgx2(em):
                for r in range(8):
                    em.emit('y', lambda y, r=r: y.dma_start(
                        tap[r * 512:(r + 1) * 512, :], sc_x2[r * 512:(r + 1) * 512, :]),
                        [("st", 16)])
                em.wait('y', "st")
            run_phase(nc, ctrs, ph_dbgx2)

        if stop_after == "wo":
            _finish(nc, ctrs, out_h, big)
            return nc, dbg_h

        wrow_ctx.close()

        bigI = big[:].bitcast(I16)
        q_pkb = [bigI[:, 0:2048], bigI[:, 2048:4096]]
        q_gb = bigI[:, 8192:12288]
        q_sb = bigI[:, 12288:16384]
        q_mg2 = big[:, 16384:20480]
        q_mg = big[:, 20480:24576]
        q_ws = big[:, 24576:28672]
        q_wsc2 = [big[:, 28672:32768], big[:, 45056:49152]]
        q_wt = big[:, 32768:40960]
        q_sx = big[:, 40960:45056]
        a0_done = {}

        with ExitStack() as qctx:
            qe = qctx.enter_context
            pst = qe(nc.psum_tensor("pst_q", [128, 2048], BF16))
            scl = qe(nc.sbuf_tensor("scl", [128, 1024], F32))
            scl4 = qe(nc.sbuf_tensor("scl4", [128, 1024], F32))

            class Chain:
                __slots__ = ("ops", "pos", "i")
                def __init__(self):
                    self.ops = []
                    self.pos = None
                    self.i = 0

            def decode_chain(pkb_ap, which_half, gb, sb, mg2, mg, ws, sx_ap, out_ap):
                ch = Chain()
                if which_half == 0:
                    ch.ops.append(('v', lambda v, a=pkb_ap, o=gb: v.tensor_scalar(
                        o, a, 6, 0x3C0, Op.logical_shift_left, Op.bitwise_and)))
                else:
                    ch.ops.append(('v', lambda v, a=pkb_ap, o=gb: v.tensor_scalar(
                        o, a, 4, 6, Op.logical_shift_right, Op.logical_shift_left)))
                ch.ops.append(('v', lambda v, i=gb, o=sb: v.tensor_scalar(
                    o, i, 0x200, 6, Op.bitwise_and, Op.logical_shift_left)))
                ch.ops.append(('v', lambda v, i=gb: v.tensor_scalar(
                    i, i, 0x1C0, 0x3E00, Op.bitwise_and, Op.bitwise_or)))
                ch.ops.append(('a', lambda a, i=gb, o=mg2: a.activation(
                    o, i.bitcast(BF16), AF.Identity,
                    scale=cmisc[:, MC_TWO:MC_TWO + 1], bias=cmisc[:, MC_NQB:MC_NQB + 1])))
                ch.ops.append(('v', lambda v, i=gb, m2=mg2, o=mg: v.tensor_tensor(
                    o, i.bitcast(BF16), m2, Op.min)))
                ch.ops.append(('v', lambda v, m=mg, sb_=sb, o=ws: v.tensor_tensor(
                    o.bitcast(I16), m.bitcast(I16), sb_, Op.bitwise_or)))
                ch.ops.append(('v', lambda v, w=ws, sx=sx_ap, o=out_ap: v.tensor_tensor(
                    o, w, sx, Op.mult)))
                return ch

            _sem_of = {'v': "v0", 'a': "a0"}

            def run_chains(em, chains):
                live = True
                while live:
                    live = False
                    for ch in chains:
                        if ch.i < len(ch.ops):
                            live = True
                            tag, fn = ch.ops[ch.i]
                            if ch.pos is not None:
                                em.wait(tag, ch.pos[0], ch.pos[1])
                            em.emit(tag, fn, [(_sem_of[tag], 1)])
                            ch.pos = (_sem_of[tag], em.count(_sem_of[tag]))
                            ch.i += 1

            v0_done = {}
            a0_grp = []

            def emit_groups(em, n_groups, src_par, wt_off, spsem):
                for grp in range(n_groups):
                    pb = grp % 2
                    if len(a0_grp) >= 2:
                        em.wait('t', "a0", a0_grp[-2])
                    for i in range(8):
                        em.emit('t', lambda t, grp=grp, i=i, pb=pb, sp_=src_par: t.transpose(
                            pst[:, pb * 1024 + i * 128: pb * 1024 + (i + 1) * 128],
                            q_wsc2[sp_][:, grp * 1024 + i * 128: grp * 1024 + (i + 1) * 128],
                            cid_bf[:]), [("t0", 1)] if i == 7 else [])
                    em.wait('a', "t0")
                    em.wait('a', spsem)
                    em.emit('a', lambda a, grp=grp, pb=pb, wo=wt_off: a.copy(
                        q_wt[:, wo + grp * 1024: wo + (grp + 1) * 1024],
                        pst[:, pb * 1024: (pb + 1) * 1024]), [("a0", 1)])
                    a0_grp.append(em.count("a0"))

            def gu_loads(em, nt):
                par = nt % 2
                lsem = "lyA" if par == 0 else "lyB"
                ssem = "lw" if par == 0 else "lx"
                if nt >= 1:
                    em.wait('g', "v0", v0_done.get(("gu", nt - 1), 0))
                    em.wait('y', "v0", v0_done.get(("gu", nt - 1), 0))
                em.emit('g', lambda g, nt=nt, par=par: g.dma_start(
                    q_pkb[par][:, 0:1024], gpk_h[nt * 128:(nt + 1) * 128, :]), [(lsem, 16)])
                em.emit('g', lambda g, nt=nt, par=par: g.dma_start(
                    q_pkb[par][:, 1024:2048], upk_h[nt * 128:(nt + 1) * 128, :]), [(lsem, 16)])
                em.emit('y', lambda y, nt=nt, par=par: y.dma_start(
                    scl[:, par * 128: par * 128 + 64], gsc_h[nt * 128:(nt + 1) * 128, :]),
                    [(ssem, 16)])
                em.emit('y', lambda y, nt=nt, par=par: y.dma_start(
                    scl[:, par * 128 + 64: par * 128 + 128], usc_h[nt * 128:(nt + 1) * 128, :]),
                    [(ssem, 16)])

            def ph_dequant_gu(em, nt):
                par = nt % 2
                lsem = "lyA" if par == 0 else "lyB"
                ssem = "lw" if par == 0 else "lx"
                if nt + 1 < NNT:
                    gu_loads(em, nt + 1)
                em.wait('v', lsem)
                em.wait('v', ssem)
                em.emit('v', lambda v, par=par: v.tensor_scalar_mul(
                    scl4[:, par * 128:par * 128 + 128], scl[:, par * 128:par * 128 + 128], 4.0),
                    [("v0", 1)])
                em.wait('v', "v0")
                for ti in range(2):
                    sc_src = scl4[:, par * 128 + ti * 64: par * 128 + (ti + 1) * 64]
                    em.emit('v', lambda v, ti=ti, sc_src=sc_src: v.tensor_copy(
                        q_sx[:, ti * 2048: ti * 2048 + 1024].rearrange("p (n o) -> p n o", o=16),
                        sc_src.rearrange("p (n o) -> p n o", o=1).broadcast_to([128, 64, 16])),
                        [("v0", 1)])
                em.wait('v', "v0")
                em.wait('v', "a0", a0_done.get(("gu", nt - 2), 0))
                q_wsc = q_wsc2[par]
                chains = []
                for ti in range(2):
                    for hf in range(2):
                        q = ti * 2 + hf
                        chains.append(decode_chain(
                            q_pkb[par][:, ti * 1024:(ti + 1) * 1024], hf,
                            q_gb[:, q * 1024:(q + 1) * 1024], q_sb[:, q * 1024:(q + 1) * 1024],
                            q_mg2[:, q * 1024:(q + 1) * 1024],
                            q_mg[:, q * 1024:(q + 1) * 1024], q_ws[:, q * 1024:(q + 1) * 1024],
                            q_sx[:, ti * 2048: ti * 2048 + 1024],
                            q_wsc[:, (ti * 2 + hf) * 1024:(ti * 2 + hf + 1) * 1024]))
                run_chains(em, chains)
                v0_done[("gu", nt)] = em.count("v0")
                em.wait('t', "v0")
                wt_off = par * 4096
                spsem = "spA" if par == 0 else "spB"
                for ti in range(2):
                    for grp in range(2):
                        pb = (ti * 2 + grp) % 2
                        if len(a0_grp) >= 2:
                            em.wait('t', "a0", a0_grp[-2])
                        for i in range(8):
                            cc = grp * 8 + i
                            c, hf = cc // 2, cc % 2
                            em.emit('t', lambda t, q_wsc=q_wsc, ti=ti, c=c, hf=hf, i=i, pb=pb: t.transpose(
                                pst[:, pb * 1024 + i * 128: pb * 1024 + (i + 1) * 128],
                                q_wsc[:, ti * 2048 + hf * 1024 + c * 128:
                                      ti * 2048 + hf * 1024 + (c + 1) * 128],
                                cid_bf[:]), [("t0", 1)] if i == 7 else [])
                        em.wait('a', "t0")
                        em.wait('a', spsem)
                        em.emit('a', lambda a, ti=ti, grp=grp, pb=pb, wo=wt_off: a.copy(
                            q_wt[:, wo + ti * 2048 + grp * 1024: wo + ti * 2048 + (grp + 1) * 1024],
                            pst[:, pb * 1024:(pb + 1) * 1024]), [("a0", 1)])
                        a0_grp.append(em.count("a0"))
                a0_done[("gu", nt)] = em.count("a0")
                em.wait('y', "a0")
                em.emit('y', lambda y, nt=nt, wo=wt_off: y.dma_start(
                    sc_wgT[nt * 128:(nt + 1) * 128, :], q_wt[:, wo:wo + 2048]), [(spsem, 16)])
                em.emit('y', lambda y, nt=nt, wo=wt_off: y.dma_start(
                    sc_wuT[nt * 128:(nt + 1) * 128, :], q_wt[:, wo + 2048:wo + 4096]), [(spsem, 16)])

            def wd_loads(em, j, sl):
                gi = j * 4 + sl
                par = gi % 2
                lsem = "lyA" if par == 0 else "lyB"
                if gi >= 1:
                    em.wait('g', "v0", v0_done.get(("wd", gi - 1), v0_done.get(("gu", NNT - 1), 0)))
                em.emit('g', lambda g, j=j, sl=sl, par=par: g.dma_start(
                    q_pkb[par][:], dpk_h[j * 128:(j + 1) * 128, sl * 2048:(sl + 1) * 2048]),
                    [(lsem, 16)])
                if sl == 0:
                    ssem = "lw" if (j % 2) == 0 else "lx"
                    em.wait('y', "v0", v0_done.get(("wdscl", j - 1),
                                                   v0_done.get(("gu", NNT - 1), 0)))
                    em.emit('y', lambda y, j=j: y.dma_start(
                        scl[:, (j % 2) * 512:(j % 2) * 512 + 512],
                        dsc_h[j * 128:(j + 1) * 128, :]), [(ssem, 16)])

            def ph_dequant_wd(em, j, sl):
                gi = j * 4 + sl
                par = gi % 2
                lsem = "lyA" if par == 0 else "lyB"
                if gi + 1 < H * 4:
                    wd_loads(em, (gi + 1) // 4, (gi + 1) % 4)
                if sl == 0:
                    ssem = "lw" if (j % 2) == 0 else "lx"
                    em.wait('v', ssem)
                    em.emit('v', lambda v, j=j: v.tensor_scalar_mul(
                        scl4[:, (j % 2) * 512:(j % 2) * 512 + 512],
                        scl[:, (j % 2) * 512:(j % 2) * 512 + 512], 4.0), [("v0", 1)])
                    v0_done[("wdscl", j)] = em.count("v0")
                    em.wait('v', "v0")
                em.wait('v', lsem)
                sc_src = scl4[:, (j % 2) * 512 + sl * 128:(j % 2) * 512 + (sl + 1) * 128]
                em.emit('v', lambda v, sc_src=sc_src: v.tensor_copy(
                    q_sx[:, 0:2048].rearrange("p (n o) -> p n o", o=16),
                    sc_src.rearrange("p (n o) -> p n o", o=1).broadcast_to([128, 128, 16])),
                    [("v0", 1)])
                em.wait('v', "v0")
                em.wait('v', "a0", a0_done.get(("wd", gi - 2), a0_done.get(("gu", NNT - 1), 0)))
                q_wsc = q_wsc2[par]
                ev = q_wsc.rearrange("p (j two) -> p j two", two=2)[:, :, 0]
                od = q_wsc.rearrange("p (j two) -> p j two", two=2)[:, :, 1]
                chains = []
                for hf, out_ap in ((0, ev), (1, od)):
                    chains.append(decode_chain(
                        q_pkb[par][:], hf,
                        q_gb[:, hf * 2048:(hf + 1) * 2048], q_sb[:, hf * 2048:(hf + 1) * 2048],
                        q_mg2[:, hf * 2048:(hf + 1) * 2048], q_mg[:, hf * 2048:(hf + 1) * 2048],
                        q_ws[:, hf * 2048:(hf + 1) * 2048], q_sx[:, 0:2048], out_ap))
                run_chains(em, chains)
                v0_done[("wd", gi)] = em.count("v0")
                em.wait('t', "v0")
                wt_off = par * 4096
                spsem = "spA" if par == 0 else "spB"
                emit_groups(em, 4, par, wt_off, spsem)
                a0_done[("wd", gi)] = em.count("a0")
                em.wait('y', "a0")
                base = gi * 128
                em.emit('y', lambda y, base=base, wo=wt_off: y.dma_start(
                    sc_wdT[base:base + 128, :], q_wt[:, wo:wo + 4096]), [(spsem, 16)])

            def ph_dequant(em):
                fence(em)
                gu_loads(em, 0)
                for nt in range(NNT):
                    ph_dequant_gu(em, nt)
                wd_loads(em, 0, 0)
                for j in range(H):
                    for sl in range(4):
                        ph_dequant_wd(em, j, sl)
            run_phase(nc, ctrs, ph_dequant)

        if "wgt" in dbg:
            tap = dbg_tensor("wgt", [MLP, D], BF16)
            def ph_dbgw(em):
                for r in range(16):
                    em.emit('y', lambda y, r=r: y.dma_start(
                        tap[r * 1024:(r + 1) * 1024, :], sc_wgT[r * 1024:(r + 1) * 1024, :]),
                        [("st", 16)])
                em.wait('y', "st")
            run_phase(nc, ctrs, ph_dbgw)
        if "wdt" in dbg:
            tap = dbg_tensor("wdt", [8192, 4096], BF16)
            def ph_dbgwd(em):
                for r in range(16):
                    em.emit('y', lambda y, r=r: y.dma_start(
                        tap[r * 512:(r + 1) * 512, :], sc_wdT[r * 512:(r + 1) * 512, :]),
                        [("st", 16)])
                em.wait('y', "st")
            run_phase(nc, ctrs, ph_dbgwd)

        if stop_after == "dequant":
            _finish(nc, ctrs, out_h, big)
            return nc, dbg_h

        ln3xT = e(nc.sbuf_tensor("ln3xT", [128, 16 * 512], BF16))
        v0_hist = {}
        a0_hist = {}
        for tc in range(8):
            with ExitStack() as fa:
                fe = fa.enter_context
                pst = fe(nc.psum_tensor(f"pst_f_{tc}", [128, 2048], BF16))
                x2in = fe(nc.sbuf_tensor(f"x2in_{tc}", [128, 2 * 2048], F32))
                ln3o = fe(nc.sbuf_tensor(f"ln3o_{tc}", [128, 2 * 2048], BF16))

                def ph_ln3(em, tc=tc):
                    fence(em)
                    for u in range(4):
                        b = u % 2
                        lsem = "lyA" if b == 0 else "lyB"
                        xv = x2in[:, b * 2048:(b + 1) * 2048]
                        lv = ln3o[:, b * 2048:(b + 1) * 2048]
                        em.wait('y', "v0")
                        row0 = tc * 512 + u * 128
                        em.emit('y', lambda y, row0=row0, xv=xv: y.dma_start(
                            xv, sc_x2[row0:row0 + 128, :]), [(lsem, 16)])
                        em.wait('v', lsem)
                        em.wait('a', lsem)
                        ln_stats(em, xv, rstd_col=4, nmean_col=1, n_elems=D, sq_slice=sq_scr[:],
                                 sqcol=5 + (u % 2) * 2)
                        em.wait('v', "a0")
                        em.wait('v', "t0")
                        em.emit('v', lambda v, xv=xv, lv=lv: v.tensor_scalar(
                            lv, xv, stat[:, 1:2], stat[:, 4:5], Op.add, Op.mult), [("v0", 1)])
                        em.wait('t', "v0")
                        for cc in range(16):
                            c, par = cc // 2, cc % 2
                            pb = cc % 2
                            a0b = em.count("a0")
                            em.wait('t', "a0", a0b if (u == 0 and cc == 0) else max(0, a0b - 1))
                            em.emit('t', lambda t, lv=lv, c=c, par=par, pb=pb: t.transpose(
                                pst[:, pb * 1024:pb * 1024 + 128],
                                lv[:, 256 * c + par: 256 * (c + 1): 2], cid_bf[:]), [("t0", 1)])
                            em.wait('a', "t0")
                            em.emit('a', lambda a, cc=cc, u=u, pb=pb: a.copy(
                                ln3xT[:, cc * 512 + u * 128: cc * 512 + (u + 1) * 128],
                                pst[:, pb * 1024:pb * 1024 + 128]), [("a0", 1)])
                run_phase(nc, ctrs, ph_ln3)

            with ExitStack() as fb:
                fe = fb.enter_context
                wgbuf = fe(nc.sbuf_tensor(f"wgbuf_{tc}", [128, 2 * 2048], BF16))
                wubuf = fe(nc.sbuf_tensor(f"wubuf_{tc}", [128, 2 * 2048], BF16))
                gsc = fe(nc.sbuf_tensor(f"gsc_{tc}", [128, 4096], F32))
                ps_g = fe(nc.psum_tensor(f"ps_g_{tc}", [128, 1024], F32))
                ps_u = fe(nc.psum_tensor(f"ps_u_{tc}", [128, 1024], F32))

                def ph_gateup(em, tc=tc):
                    fence(em)
                    v0_gelu = {}
                    t1_0 = em.count("t1")
                    for nt in range(NNT):
                        rb = nt % 2
                        for pf in ([0, 1] if nt == 0 else [nt + 1]):
                            if pf < NNT:
                                fsem = "fwA" if pf % 2 == 0 else "fwB"
                                if pf >= 2:
                                    em.wait('y', "t1", t1_0 + pf - 1)
                                em.emit('y', lambda y, pf=pf: y.dma_start(
                                    wgbuf[:, (pf % 2) * 2048:(pf % 2 + 1) * 2048],
                                    sc_wgT[pf * 128:(pf + 1) * 128, :]), [(fsem, 16)])
                                em.emit('y', lambda y, pf=pf: y.dma_start(
                                    wubuf[:, (pf % 2) * 2048:(pf % 2 + 1) * 2048],
                                    sc_wuT[pf * 128:(pf + 1) * 128, :]), [(fsem, 16)])
                        em.wait('t', "fwA" if rb == 0 else "fwB")
                        if nt >= 2:
                            em.wait('t', "v0", v0_gelu[nt - 2])
                        for cc in range(16):
                            em.emit('t', lambda t, cc=cc, rb=rb: t.matmul(
                                ps_g[:, rb * 512:rb * 512 + 512],
                                wgbuf[:, rb * 2048 + cc * 128: rb * 2048 + (cc + 1) * 128],
                                ln3xT[:, cc * 512:(cc + 1) * 512],
                                start=(cc == 0), stop=(cc == 15)), [])
                        for cc in range(16):
                            em.emit('t', lambda t, cc=cc, rb=rb: t.matmul(
                                ps_u[:, rb * 512:rb * 512 + 512],
                                wubuf[:, rb * 2048 + cc * 128: rb * 2048 + (cc + 1) * 128],
                                ln3xT[:, cc * 512:(cc + 1) * 512],
                                start=(cc == 0), stop=(cc == 15)),
                                [("t1", 1)] if cc == 15 else [])
                        gb_ = (nt % 2) * 2048
                        s0 = gsc[:, gb_ + 0:gb_ + 512]
                        s1 = gsc[:, gb_ + 512:gb_ + 1024]
                        s2 = gsc[:, gb_ + 1024:gb_ + 1536]
                        s3 = gsc[:, gb_ + 1536:gb_ + 2048]
                        em.wait('a', "t1")
                        em.emit('a', lambda a, rb=rb: a.activation(s0, ps_g[:, rb * 512:rb * 512 + 512],
                                                                   AF.Square), [("a0", 1)])
                        em.wait('v', "a0")
                        em.emit('v', lambda v: v.tensor_scalar(s1, s0, GC2, GC1, Op.mult, Op.add),
                                [("v0", 1)])
                        em.wait('v', "v0")
                        em.emit('v', lambda v, rb=rb: v.tensor_tensor(
                            s2, ps_g[:, rb * 512:rb * 512 + 512], s1, Op.mult), [("v0", 1)])
                        em.wait('a', "v0")
                        em.emit('a', lambda a: a.activation(s3, s2, AF.Tanh), [("a0", 1)])
                        em.wait('v', "a0")
                        em.emit('v', lambda v: v.tensor_scalar(s1, s3, 0.5, 0.5, Op.mult, Op.add),
                                [("v0", 1)])
                        em.wait('v', "v0")
                        em.emit('v', lambda v, rb=rb: v.tensor_tensor(
                            s2, ps_g[:, rb * 512:rb * 512 + 512], s1, Op.mult), [("v0", 1)])
                        em.wait('v', "v0")
                        em.emit('v', lambda v, rb=rb, nt=nt: v.tensor_tensor(
                            big[:, nt * 512:(nt + 1) * 512], ps_u[:, rb * 512:rb * 512 + 512],
                            s2, Op.mult), [("v0", 1)])
                        v0_gelu[nt] = em.count("v0")
                run_phase(nc, ctrs, ph_gateup)

            with ExitStack() as fc:
                fe = fc.enter_context
                wdring = fe(nc.sbuf_tensor(f"wdring_{tc}", [128, 3 * 2048], BF16))
                dstage = fe(nc.sbuf_tensor(f"dstage_{tc}", [128, 2 * 512], F32))
                xo2 = fe(nc.sbuf_tensor(f"xo2_{tc}", [128, 2 * 2048], F32))
                x2res = fe(nc.sbuf_tensor(f"x2res_{tc}", [128, 2048], F32))
                outt = fe(nc.sbuf_tensor(f"outt_{tc}", [128, 2048], F32))
                ps_d = fe(nc.psum_tensor(f"ps_d_{tc}", [128, 1024], F32))
                ps_t2 = fe(nc.psum_tensor(f"ps_t2_{tc}", [128, 1024], F32))

                def ph_down(em, tc=tc):
                    tmpn2 = sq_scr
                    ln3f = ln3xT[:].bitcast(F32)
                    def xo_ap(u, lo, hi):
                        if u < 2:
                            return ln3f[:, u * 2048 + lo: u * 2048 + hi]
                        return xo2[:, (u - 2) * 2048 + lo: (u - 2) * 2048 + hi]
                    a0_evac = {}
                    t1_0 = em.count("t1")
                    fence(em)
                    for j in range(H):
                        jb = j % 2
                        for bt in range(8):
                            gi = j * 8 + bt
                            for pf in ([0, 1, 2] if gi == 0 else [gi + 2]):
                                if pf < H * 8:
                                    pj, pbt = pf // 8, pf % 8
                                    ps_ = pj * 4 + pbt // 2
                                    dsem = ("dwA", "dwB", "dwC")[pf % 3]
                                    if pf >= 3:
                                        em.wait('y', "t1", t1_0 + pf - 2)
                                    em.emit('y', lambda y, ps_=ps_, pbt=pbt, pf=pf: y.dma_start(
                                        wdring[:, (pf % 3) * 2048:(pf % 3 + 1) * 2048],
                                        sc_wdT[ps_ * 128:(ps_ + 1) * 128,
                                               (pbt % 2) * 2048:(pbt % 2) * 2048 + 2048]),
                                        [(dsem, 16)])
                            em.wait('t', ("dwA", "dwB", "dwC")[gi % 3])
                            if j >= 2 and bt == 0:
                                em.wait('t', "a0", a0_evac[j - 2])
                            for u in range(16):
                                i = bt * 16 + u
                                em.emit('t', lambda t, u=u, i=i, gi=gi, jb=jb: t.matmul(
                                    ps_d[:, jb * 512:jb * 512 + 512],
                                    wdring[:, (gi % 3) * 2048 + u * 128:(gi % 3) * 2048 + (u + 1) * 128],
                                    big[:, i * 512:(i + 1) * 512],
                                    start=(i == 0), stop=(i == NNT - 1)),
                                    [("t1", 1)] if u == 15 else [])
                        em.wait('a', "t1")
                        em.emit('a', lambda a, jb=jb: a.copy(
                            dstage[:, jb * 512:(jb + 1) * 512], ps_d[:, jb * 512:jb * 512 + 512]),
                            [("a0", 1)])
                        a0_evac[j] = em.count("a0")
                        for u in range(4):
                            pb = u % 2
                            v0b = em.count("v0")
                            em.wait('t', "v0", v0b if (j == 0 and u == 0) else max(0, v0b - 1))
                            em.wait('t', "a0")
                            em.emit('t', lambda t, u=u, jb=jb, pb=pb: t.transpose(
                                ps_t2[:, pb * 512:pb * 512 + 128],
                                dstage[:, jb * 512 + u * 128: jb * 512 + (u + 1) * 128], cid_f[:]),
                                [("t0", 1)])
                            em.wait('v', "t0")
                            em.emit('v', lambda v, u=u, j=j, pb=pb: v.tensor_copy(
                                xo_ap(u, j * 128, (j + 1) * 128),
                                ps_t2[:, pb * 512:pb * 512 + 128]), [("v0", 1)])
                    for u in range(4):
                        row0 = tc * 512 + u * 128
                        xo = xo_ap(u, 0, 2048)
                        em.wait('y', "v0")
                        em.emit('y', lambda y, row0=row0: y.dma_start(
                            x2res[:], sc_x2[row0:row0 + 128, :]), [("lx", 16)])
                        em.wait('v', "v0")
                        em.wait('a', "v0")
                        ln_stats(em, xo, rstd_col=4, nmean_col=1, n_elems=D, sq_slice=sq_scr[:],
                                 sqcol=5 + (u % 2) * 2)
                        em.wait('v', "a0")
                        em.emit('v', lambda v, xo=xo: v.tensor_scalar(
                            tmpn2[:, 0:2048], xo, stat[:, 1:2], stat[:, 4:5], Op.add, Op.mult), [("v0", 1)])
                        em.wait('v', "lx")
                        em.wait('v', "st")
                        em.wait('v', "v0")
                        em.emit('v', lambda v: v.tensor_add(outt[:], tmpn2[:, 0:2048], x2res[:]), [("v0", 1)])
                        em.wait('y', "v0")
                        em.emit('y', lambda y, row0=row0: y.dma_start(
                            out_h[row0:row0 + 128, :], outt[:]), [("st", 16)])
                run_phase(nc, ctrs, ph_down)

        def ph_fin(em):
            em.wait('y', "st")
            em.wait('t', "st")
        run_phase(nc, ctrs, ph_fin)
        return nc, dbg_h


def _finish(nc, ctrs, out_h, big):
    def ph(em):
        em.emit('y', lambda y: y.dma_start(out_h[0:128, :], big[:, 0:4096].bitcast(F32)),
                [("st", 16)])
        em.wait('y', "st")
    run_phase(nc, ctrs, ph)


def _consts():
    cid_bf = np.eye(128, dtype=BF16NP)
    cid_f = np.eye(128, dtype=np.float32)
    p = np.arange(128, dtype=np.float32)[:, None]
    j = np.arange(512, dtype=np.float32)[None, :]
    r0 = p - j
    planes = [r0]
    for c in (0.0, 128.0, 256.0, 384.0):
        m = np.where(r0 + c > 0, np.float32(-1e33), np.float32(0.0))
        planes.append(r0 + m)
    cr0 = np.concatenate(planes, axis=1).astype(np.float32)
    def mplane(c):
        return np.where(r0 + c > 0, np.float32(-1e33), np.float32(0.0))
    pl = np.concatenate([r0, r0 + 128.0, r0 + mplane(0.0), r0 + 128.0 + mplane(128.0),
                         r0 + mplane(256.0), r0 + 128.0 + mplane(384.0)], axis=1).astype(np.float32)
    hi = pl.astype(BF16NP)
    lo = (pl - hi.astype(np.float32)).astype(BF16NP)
    cr3 = np.concatenate([hi, lo], axis=1)
    cdg = np.zeros((128, H * 128), dtype=BF16NP)
    for h in range(H):
        cdg[:, h * 128:(h + 1) * 128] = (np.eye(128, dtype=np.float32)
                                         * np.float32(ALIBI_SLOPES[h])).astype(BF16NP)
    ceb = np.zeros((128, H * 16), dtype=np.float32)
    for h in range(H):
        for idx in range(16):
            coff = np.float32(128.0) * np.float32(idx - 12)
            ceb[:, h * 16 + idx] = np.float32(ALIBI_SLOPES[h]) * coff
    cones = np.ones((128, 1), dtype=BF16NP)
    conesrow = np.ones((1, 128), dtype=BF16NP)
    cmisc = np.zeros((128, NMISC), dtype=np.float32)
    cmisc[:, MC_EPS] = EPS
    cmisc[:, MC_NHALF] = -0.5
    cmisc[:, MC_QS] = QS
    cmisc[:, MC_TWO] = 2.0
    cmisc[:, MC_NQB] = -0.25
    return dict(cid_bf=cid_bf, cid_f=cid_f, cr0=cr0, cr3=cr3, cdg=cdg, ceb=ceb, cones=cones,
                conesrow=conesrow, cmisc=cmisc)


def _pack_u8(a_int32):
    a = np.ascontiguousarray(a_int32)
    return np.ascontiguousarray(a.view(np.uint8).reshape(a.shape[0], a.shape[1], 4)[:, :, 0])


def make_in_maps(L0_attn_x, L0_attn_Wq, L0_attn_Wk, L0_attn_Wv, L0_attn_Wo,
                 L0_ffn_gate_W_packed, L0_ffn_gate_scales, L0_ffn_up_W_packed,
                 L0_ffn_up_scales, L0_ffn_down_W_packed, L0_ffn_down_scales):
    consts = _consts()
    shared = dict(
        wq=np.ascontiguousarray(L0_attn_Wq, dtype=np.float32),
        wk=np.ascontiguousarray(L0_attn_Wk, dtype=np.float32),
        wv=np.ascontiguousarray(L0_attn_Wv, dtype=np.float32),
        wo=np.ascontiguousarray(L0_attn_Wo, dtype=np.float32),
        gpk=_pack_u8(L0_ffn_gate_W_packed),
        gsc=np.ascontiguousarray(L0_ffn_gate_scales, dtype=np.float32),
        upk=_pack_u8(L0_ffn_up_W_packed),
        usc=np.ascontiguousarray(L0_ffn_up_scales, dtype=np.float32),
        dpk=_pack_u8(L0_ffn_down_W_packed),
        dsc=np.ascontiguousarray(L0_ffn_down_scales, dtype=np.float32),
        **consts,
    )
    x = np.ascontiguousarray(L0_attn_x, dtype=np.float32).reshape(16, T, D)
    in_maps = []
    for c in range(8):
        m = dict(shared)
        m["x"] = np.ascontiguousarray(x[2 * c:2 * c + 2].reshape(TOK, D))
        in_maps.append(m)
    return in_maps


def kernel(**inputs):
    in_maps = make_in_maps(**inputs)
    nc, _ = build()
    res = run_bass_kernel_spmd(nc, in_maps, core_ids=list(range(8)))
    out = np.concatenate([r["out"].reshape(2, T, D) for r in res.results], axis=0)
    return out
